# version 11 (final)
# speedup vs baseline: 1.5333x; 1.0294x over previous
import os
import numpy as np
from contextlib import ExitStack, contextmanager

import concourse.bass as bass
import concourse.mybir as mybir
from concourse.bass_utils import run_bass_kernel_spmd

F32 = mybir.dt.float32
BF16 = mybir.dt.bfloat16
I32 = mybir.dt.int32
I16 = mybir.dt.int16
AF = mybir.ActivationFunctionType
ALU = mybir.AluOpType
AX = mybir.AxisListType

T = 2048
D = 1024
SERIALIZE_EXP = True
NT = T // 128
NC = D // 128


def bcast16(ap2d):
    return bass.AP(ap2d.tensor, ap2d.offset, list(ap2d.ap) + [[0, 16]])


def view3(ap2d, inner=16):
    a0, a1 = ap2d.ap
    assert a1[0] == 1
    return bass.AP(ap2d.tensor, ap2d.offset, [a0, [inner, a1[1] // inner], [1, inner]])


class Ctr:
    def __init__(self, sem):
        self.sem = sem
        self.v = 0

    def inc(self, n):
        self.v += n
        return self.v



_GUARDED_OPS = {
    "tensor_tensor", "tensor_scalar", "tensor_copy", "tensor_reduce",
    "reduce_max", "reduce_sum", "transpose", "memset", "select",
    "scalar_tensor_tensor", "tensor_tensor_scan", "iota", "affine_select",
    "activation", "copy", "mul", "add", "sign", "sqrt", "square", "memzero",
    "bn_stats", "bn_aggr", "stream_shuffle", "copy_predicated", "tensor_add",
    "tensor_sub", "tensor_mul", "tensor_max", "tensor_relu",
    "tensor_scalar_add", "tensor_scalar_mul", "tensor_scalar_max",
    "tensor_scalar_min", "tensor_scalar_sub", "tensor_single_scalar",
}


class EngGuard:

    def __init__(self, eng, ctr):
        self._eng = eng
        self._ctr = ctr

    def __getattr__(self, name):
        attr = getattr(self._eng, name)
        if name not in _GUARDED_OPS:
            return attr
        ctr, eng = self._ctr, self._eng

        def wrapped(*args, **kw):
            eng.wait_ge(ctr.sem, ctr.v)
            inst = attr(*args, **kw)
            inst.then_inc(ctr.sem, 1)
            ctr.inc(1)
            return inst
        return wrapped


class Prog:
    def __init__(self):
        self.nc = bass.Bass("TRN2", target_bir_lowering=False,
                            detect_race_conditions=os.environ.get("MKRACES", "0") == "1")
        self.es = ExitStack()
        self._sems = {}
        self._names = {}

    def sem(self, name):
        if name not in self._sems:
            self._sems[name] = Ctr(self.es.enter_context(self.nc.semaphore(name)))
        return self._sems[name]

    def _uniq(self, name):
        n = self._names.get(name, 0)
        self._names[name] = n + 1
        return name if n == 0 else f"{name}__{n}"

    def sb(self, name, shape, dtype):
        return self.es.enter_context(self.nc.sbuf_tensor(self._uniq(name), shape, dtype))

    def ps(self, name, shape, dtype=F32):
        return self.es.enter_context(self.nc.psum_tensor(self._uniq(name), shape, dtype))

    def dram(self, name, shape, dtype, kind="Internal"):
        return self.nc.dram_tensor(self._uniq(name), shape, dtype, kind=kind).ap()

    @contextmanager
    def scope(self):
        es = ExitStack()
        old = self.es
        self.es = es
        try:
            with es:
                yield
        finally:
            self.es = old

    def join(self, engines=("tE", "vE", "sE", "gE", "yE")):
        for en in engines:
            e = getattr(self, en)
            for name, c in self._sems.items():
                if c.v:
                    e.wait_ge(c.sem, c.v)

    @property
    def tE(self):
        return self.nc.tensor

    @property
    def vE(self):
        return EngGuard(self.nc.vector, self.sem("ve"))

    @property
    def sE(self):
        return EngGuard(self.nc.scalar, self.sem("ac"))

    @property
    def gE(self):
        return EngGuard(self.nc.gpsimd, self.sem("po"))

    @property
    def yE(self):
        return self.nc.sync

    def consts(self):
        g = self.nc.gpsimd
        self.id_f32 = self.sb("c_id_f32", [128, 128], F32)
        self.id_bf = self.sb("c_id_bf", [128, 128], BF16)
        self.tril_s = self.sb("c_tril_s", [128, 128], BF16)
        self.tril_i = self.sb("c_tril_i", [128, 128], BF16)
        self.triu_i = self.sb("c_triu_i", [128, 128], BF16)
        self.negm = self.sb("c_negm", [128, 128], F32)
        self.ones_col_bf = self.sb("c_ones_col", [128, 1], BF16)
        self.ones_col_f32 = self.sb("c_ones_colf", [128, 1], F32)
        self.ones_p1_f32 = self.sb("c_ones_p1f", [1, 128], F32)
        self.ones_p1_bf = self.sb("c_ones_p1b", [1, 128], BF16)
        for t_, fill in [(self.id_f32, 1.0), (self.id_bf, 1.0)]:
            g.memset(t_[:], 0.0)
            g.affine_select(out=t_[:], in_=t_[:], compare_op=ALU.not_equal, fill=fill,
                            base=0, pattern=[[-1, 128]], channel_multiplier=1)
        g.memset(self.tril_s[:], 1.0)
        g.affine_select(out=self.tril_s[:], in_=self.tril_s[:], compare_op=ALU.is_ge,
                        fill=0.0, base=-1, pattern=[[-1, 128]], channel_multiplier=1)
        g.memset(self.tril_i[:], 1.0)
        g.affine_select(out=self.tril_i[:], in_=self.tril_i[:], compare_op=ALU.is_ge,
                        fill=0.0, base=0, pattern=[[-1, 128]], channel_multiplier=1)
        g.memset(self.triu_i[:], 1.0)
        g.affine_select(out=self.triu_i[:], in_=self.triu_i[:], compare_op=ALU.is_ge,
                        fill=0.0, base=0, pattern=[[1, 128]], channel_multiplier=-1)
        g.memset(self.negm[:], 0.0)
        g.affine_select(out=self.negm[:], in_=self.negm[:], compare_op=ALU.is_ge,
                        fill=-1e30, base=0, pattern=[[-1, 128]], channel_multiplier=1)
        self.negmT = self.sb("c_negmT", [128, 128], F32)
        g.memset(self.negmT[:], 0.0)
        g.affine_select(out=self.negmT[:], in_=self.negmT[:], compare_op=ALU.is_ge,
                        fill=-1e30, base=0, pattern=[[1, 128]], channel_multiplier=-1)
        g.memset(self.ones_col_bf[:], 1.0)
        g.memset(self.ones_col_f32[:], 1.0)
        g.memset(self.ones_p1_f32[:], 1.0)
        g.memset(self.ones_p1_bf[:], 1.0)
        self.cf = {}
        self._cf_block = self.sb("c_cf", [128, 16], F32)
        for i, (nm, val) in enumerate([("eps5", 1e-5), ("eps6", 1e-6),
                                       ("eps12", 1e-12), ("zero", 0.0),
                                       ("one", 1.0), ("neg1", -1.0),
                                       ("eps24", 1e-24)]):
            g.memset(self._cf_block[:, i:i + 1], float(val))
            self.cf[nm] = self._cf_block[:, i:i + 1]




def emit_load_x_fm(p, x_dram, x_fm, stage, pb):
    qs, mm, ve = p.sem("qs"), p.sem("mm"), p.sem("ve")
    mm_at, ve_at = [], []
    for i in range(NT):
        buf = stage[:, (i % 2) * D:(i % 2) * D + D]
        if i >= 2:
            p.yE.wait_ge(mm.sem, mm_at[i - 2])
        p.yE.dma_start(out=buf, in_=x_dram[i * 128:(i + 1) * 128, :]).then_inc(qs.sem, 16)
        qs.inc(16)
        p.tE.wait_ge(qs.sem, qs.v)
        if i >= 1:
            p.tE.wait_ge(ve.sem, ve_at[i - 1])
        for c in range(NC):
            ti = p.tE.transpose(out=pb[:, c * 128:c * 128 + 128],
                                in_=buf[:, c * 128:(c + 1) * 128], identity=p.id_f32[:])
        ti.then_inc(mm.sem, 1)
        mm.inc(1)
        mm_at.append(mm.v)
        p.vE.wait_ge(mm.sem, mm.v)
        for c in range(NC):
            inst = p.vE.tensor_copy(out=x_fm[:, c * T + i * 128:c * T + (i + 1) * 128],
                                    in_=pb[:, c * 128:c * 128 + 128])
        inst
        ve_at.append(ve.v)


def emit_store_fm(p, src_fm, out_dram, stage, pb):
    mm, ve, so = p.sem("mm"), p.sem("ve"), p.sem("so")
    so_at, ve_at = [], []
    for i in range(NT):
        buf = stage[:, (i % 2) * D:(i % 2) * D + D]
        if i >= 2:
            p.vE.wait_ge(so.sem, so_at[i - 2])
        if i >= 1:
            p.tE.wait_ge(ve.sem, ve_at[i - 1])
        for c in range(NC):
            ti = p.tE.transpose(out=pb[:, c * 128:c * 128 + 128],
                                in_=src_fm[:, c * T + i * 128:c * T + (i + 1) * 128],
                                identity=p.id_f32[:])
        ti.then_inc(mm.sem, 1)
        mm.inc(1)
        p.vE.wait_ge(mm.sem, mm.v)
        for c in range(NC):
            inst = p.vE.tensor_copy(out=buf[:, c * 128:(c + 1) * 128],
                                    in_=pb[:, c * 128:c * 128 + 128])
        inst
        ve_at.append(ve.v)
        p.yE.wait_ge(ve.sem, ve.v)
        p.yE.dma_start(out=out_dram[i * 128:(i + 1) * 128, :], in_=buf).then_inc(so.sem, 16)
        so.inc(16)
        so_at.append(so.v)


def emit_cast_prepass(p, src_dram, dst_dram, rows, cols, stage):
    qg, so = p.sem("qg"), p.sem("so")
    so_at = {}
    for i in range(rows // 128):
        buf = stage[:, (i % 2) * cols:(i % 2) * cols + cols]
        if i >= 2:
            p.gE.wait_ge(so.sem, so_at[i - 2])
        p.gE.dma_start(out=buf, in_=src_dram[i * 128:(i + 1) * 128, :]).then_inc(qg.sem, 16)
        qg.inc(16)
        p.yE.wait_ge(qg.sem, qg.v)
        p.yE.dma_start(out=dst_dram[i * 128:(i + 1) * 128, :], in_=buf).then_inc(so.sem, 16)
        so.inc(16)
        so_at[i] = so.v



def emit_fp4_quant(p, src32, dst_bf, w, chan_transpose=True):
    ve, ac = p.sem("ve"), p.sem("ac")
    F = src32.shape[1]
    NB = F // 16
    bm, sc = w["q_bm"][:, 0:NB], w["q_sc"][:, 0:NB]
    i1, i2, i3 = w["q_i1"][:, 0:F], w["q_i2"][:, 0:F], w["q_i3"][:, 0:F]
    y = w["q_y"][:, 0:F]
    bfA, bfB, bfC = w["q_bfA"][:, 0:F], w["q_bfB"][:, 0:F], w["q_bfC"][:, 0:F]
    if chan_transpose:
        p.vE.transpose(out=y, in_=src32)
    else:
        p.vE.tensor_copy(out=y, in_=src32)
    p.vE.tensor_reduce(out=bm, in_=view3(y), axis=AX.X, op=ALU.max,
                       apply_absolute_value=True)
    p.vE.tensor_scalar(out=sc, in0=bm, scalar1=float(1.0 / 6.0),
                       scalar2=float(1e-12), op0=ALU.mult,
                       op1=ALU.max)
    p.sE.wait_ge(ve.sem, ve.v)
    p.sE.activation(out=bm, in_=sc, func=AF.Ln)
    p.sE.activation(out=bm, in_=bm, func=AF.Exp, scale=float(-1.0))
    p.vE.wait_ge(ac.sem, ac.v)
    p.vE.tensor_tensor(out=view3(y), in0=view3(y), in1=bcast16(bm), op=ALU.mult)
    p.sE.wait_ge(ve.sem, ve.v)
    p.sE.activation(out=i1, in_=y, func=AF.Copy, scale=float(2.0))
    p.sE.activation(out=i2, in_=y, func=AF.Copy)
    p.sE.activation(out=i3, in_=y, func=AF.Copy, scale=float(0.5))
    p.vE.wait_ge(ac.sem, ac.v)
    p.vE.tensor_scalar(out=bfA, in0=i1, scalar1=float(0.5), scalar2=float(2.0),
                       op0=ALU.mult, op1=ALU.min)
    p.vE.tensor_scalar(out=bfA, in0=bfA, scalar1=float(-2.0), scalar2=None, op0=ALU.max)
    p.vE.tensor_scalar(out=bfB, in0=i2, scalar1=float(4.0), scalar2=float(-4.0),
                       op0=ALU.min, op1=ALU.max)
    p.vE.tensor_scalar(out=bfC, in0=i2, scalar1=float(2.0), scalar2=float(-2.0),
                       op0=ALU.min, op1=ALU.max)
    p.vE.tensor_tensor(out=bfB, in0=bfB, in1=bfC, op=ALU.subtract)
    p.vE.tensor_tensor(out=bfB, in0=bfB, in1=bfA, op=ALU.add)
    p.vE.tensor_scalar(out=bfC, in0=i3, scalar1=float(2.0), scalar2=float(-2.0),
                       op0=ALU.min, op1=ALU.max)
    p.vE.tensor_scalar(out=bfA, in0=i3, scalar1=float(1.0), scalar2=None, op0=ALU.mult)
    p.vE.tensor_tensor(out=bfA, in0=bfA, in1=bfC, op=ALU.subtract)
    p.vE.tensor_scalar(out=bfA, in0=bfA, scalar1=float(2.0), scalar2=None, op0=ALU.mult)
    p.vE.tensor_tensor(out=bfB, in0=bfB, in1=bfA, op=ALU.add)
    if chan_transpose:
        p.vE.tensor_tensor(out=view3(bfA), in0=view3(bfB), in1=bcast16(sc), op=ALU.mult)
        p.vE.transpose(out=dst_bf, in_=bfA)
    else:
        p.vE.tensor_tensor(out=view3(dst_bf), in0=view3(bfB), in1=bcast16(sc),
                           op=ALU.mult)


def make_quant_work(p, F=2048, reuseB=None, reuseC=None):
    return {
        "q_bm": p.sb("q_bm", [128, F // 16], F32)[:],
        "q_sc": p.sb("q_sc", [128, F // 16], F32)[:],
        "q_i1": p.sb("q_i1", [128, F], I16)[:],
        "q_i2": p.sb("q_i2", [128, F], I16)[:],
        "q_i3": p.sb("q_i3", [128, F], I16)[:],
        "q_y": p.sb("q_y", [128, F], F32)[:],
        "q_bfA": p.sb("q_bfA", [128, F], BF16)[:],
        "q_bfB": (reuseB if reuseB is not None else p.sb("q_bfB", [128, F], BF16)[:]),
        "q_bfC": (reuseC if reuseC is not None else p.sb("q_bfC", [128, F], BF16)[:]),
    }



def emit_ln(p, x_fm, xnb, w, quant_to=None, router=None):
    mm, ve, ac = p.sem("mm"), p.sem("ve"), p.sem("ac")
    xb, sq = w["xb"], w["sq"]
    ps_sx, ps_sxx = w["ps_sx"], w["ps_sxx"]
    sxr, sxxr = w["sxr"], w["sxxr"]
    tmpr = w["rstd_b"][0:1, 0:T]
    ps_b = w["ps_b"]
    rstd_b, mr_b = w["rstd_b"], w["mr_b"]
    xn32 = w["xn32"]
    p.vE.memset(sxr, 0.0)
    p.vE.memset(sxxr, 0.0)
    for c in range(NC):
        if c >= 1:
            p.vE.wait_ge(mm.sem, mm.v)
        p.vE.tensor_copy(out=xb, in_=x_fm[:, c * T:(c + 1) * T])
        p.sE.wait_ge(ve.sem, ve.v)
        p.sE.activation(out=sq, in_=xb, func=AF.Square)
        p.tE.wait_ge(ac.sem, ac.v)
        for sl in range(4):
            p.tE.wait_ge(ve.sem, ve.v)
            p.tE.matmul(out=ps_sx, lhsT=p.ones_col_bf[:], rhs=xb[:, sl * 512:(sl + 1) * 512],
                        start=True, stop=True)
            p.tE.matmul(out=ps_sxx, lhsT=p.ones_col_bf[:], rhs=sq[:, sl * 512:(sl + 1) * 512],
                        start=True, stop=True).then_inc(mm.sem, 1)
            mm.inc(1)
            p.vE.wait_ge(mm.sem, mm.v)
            p.vE.tensor_tensor(out=sxr[:, sl * 512:(sl + 1) * 512],
                               in0=sxr[:, sl * 512:(sl + 1) * 512], in1=ps_sx, op=ALU.add)
            p.vE.tensor_tensor(out=sxxr[:, sl * 512:(sl + 1) * 512],
                               in0=sxxr[:, sl * 512:(sl + 1) * 512], in1=ps_sxx,
                               op=ALU.add)
    p.vE.tensor_scalar(out=sxr, in0=sxr, scalar1=float(1.0 / D), scalar2=None, op0=ALU.mult)
    p.vE.tensor_scalar(out=sxxr, in0=sxxr, scalar1=float(1.0 / D), scalar2=None, op0=ALU.mult)
    p.vE.tensor_tensor(out=tmpr, in0=sxr, in1=sxr, op=ALU.mult)
    p.vE.tensor_tensor(out=sxxr, in0=sxxr, in1=tmpr, op=ALU.subtract)
    p.sE.wait_ge(ve.sem, ve.v)
    p.sE.activation(out=sxxr, in_=sxxr, func=AF.Ln, bias=p.cf['eps5'][0:1, :])
    p.sE.activation(out=sxxr, in_=sxxr, func=AF.Exp, scale=float(-0.5))
    p.vE.wait_ge(ac.sem, ac.v)
    p.vE.tensor_tensor(out=sxr, in0=sxr, in1=sxxr, op=ALU.mult)
    p.tE.wait_ge(ve.sem, ve.v)
    for src_row, dst in [(sxxr, rstd_b), (sxr, mr_b)]:
        for sl in range(4):
            p.tE.matmul(out=ps_b, lhsT=p.ones_p1_f32[:],
                        rhs=src_row[:, sl * 512:(sl + 1) * 512],
                        start=True, stop=True).then_inc(mm.sem, 1)
            mm.inc(1)
            p.vE.wait_ge(mm.sem, mm.v)
            p.vE.tensor_copy(out=dst[:, sl * 512:(sl + 1) * 512], in_=ps_b)
            p.tE.wait_ge(ve.sem, ve.v)
    for c in range(NC):
        if router is not None and c >= 1:
            p.vE.wait_ge(mm.sem, mm.v)
        p.vE.tensor_tensor(out=xn32, in0=x_fm[:, c * T:(c + 1) * T], in1=rstd_b, op=ALU.mult)
        p.vE.tensor_tensor(out=xn32, in0=xn32, in1=mr_b, op=ALU.subtract)
        p.vE.tensor_copy(out=xnb[:, c * T:(c + 1) * T], in_=xn32)
        if router is not None:
            rw_sb, lg_ps = router
            p.tE.wait_ge(ve.sem, ve.v)
            for sl in range(4):
                mi = p.tE.matmul(out=lg_ps[:, sl * 512:(sl + 1) * 512],
                                 lhsT=rw_sb[:, c * 8:(c + 1) * 8],
                                 rhs=xn32[:, sl * 512:(sl + 1) * 512],
                                 start=(c == 0), stop=(c == NC - 1))
            mi.then_inc(mm.sem, 1)
            mm.inc(1)
        if quant_to is not None:
            emit_fp4_quant(p, xn32, quant_to[:, c * T:(c + 1) * T], w)


def make_ln_work(p):
    return {
        "xb": p.sb("ln_xb", [128, T], BF16)[:],
        "sq": p.sb("ln_sq", [128, T], BF16)[:],
        "ps_sx": p.ps("ln_psx", [1, 512])[:],
        "ps_sxx": p.ps("ln_psxx", [1, 512])[:],
        "sxr": p.sb("ln_sxr", [1, T], F32)[:],
        "sxxr": p.sb("ln_sxxr", [1, T], F32)[:],
        "ps_b": p.ps("ln_psb", [128, 512])[:],
        "rstd_b": p.sb("ln_rstdb", [128, T], F32)[:],
        "mr_b": p.sb("ln_mrb", [128, T], F32)[:],
        "xn32": p.sb("ln_xn32", [128, T], F32)[:],
    }



def build_test_ln(mode="full"):
    p = Prog()
    x_dram = p.dram("x", [T, D], F32, kind="ExternalInput")
    xn_out = p.dram("xn_out", [NC * 128, T], F32, kind="ExternalOutput")
    xq_out = p.dram("xq_out", [NC * 128, T], F32, kind="ExternalOutput")
    x_fm = p.sb("x_fm", [128, NC * T], F32)
    xnb = p.sb("xnb", [128, NC * T], BF16)
    xq = p.sb("xq", [128, NC * T], BF16)
    blk = p.es.enter_context(p.nc.Block())
    with p.es:
        @blk.sync
        def _(e):
            p.consts()
            if mode == "constonly":
                ve0, so0, po0 = p.sem("ve"), p.sem("so"), p.sem("po")
                dmp0 = p.sb("dmp0", [128, T], F32)
                p.vE.wait_ge(po0.sem, po0.v)
                p.vE.tensor_copy(out=dmp0[:, 0:128], in_=p.negm[:])
                p.vE.tensor_copy(out=dmp0[:, 128:256], in_=p.id_f32[:])
                p.yE.wait_ge(ve0.sem, ve0.v)
                p.yE.dma_start(out=xn_out[0:128, :], in_=dmp0[:]).then_inc(so0.sem, 16)
                so0.inc(16)
                p.yE.wait_ge(so0.sem, so0.v)
                return
            with p.scope():
                stage = p.sb("stage", [128, 2 * D], F32)
                pb = p.ps("pb", [128, 1024])
                p.tE.wait_ge(p.sem("po").sem, p.sem("po").v)
                if mode == "dmaonly":
                    qs0, ve0, so0 = p.sem("qs"), p.sem("ve"), p.sem("so")
                    for i in range(NT):
                        buf = stage[:, (i % 2) * D:(i % 2) * D + D]
                        if i >= 2:
                            p.yE.wait_ge(ve0.sem, ve0.v)
                        p.yE.dma_start(out=buf, in_=x_dram[i * 128:(i + 1) * 128, :]).then_inc(qs0.sem, 16)
                        qs0.inc(16)
                        p.vE.wait_ge(qs0.sem, qs0.v)
                        for c in range(NC):
                            inst = p.vE.tensor_copy(
                                out=x_fm[:, c * T + i * 128:c * T + (i + 1) * 128],
                                in_=buf[:, c * 128:(c + 1) * 128])

                else:
                    emit_load_x_fm(p, x_dram, x_fm[:], stage[:], pb[:])
            p.join()
            lw = make_ln_work(p)
            lw.update(make_quant_work(p, reuseB=lw["xb"], reuseC=lw["sq"]))
            if mode not in ("loadonly", "dmaonly", "rt"):
                emit_ln(p, x_fm[:], xnb[:], lw, quant_to=(xq[:] if mode == "full" else None))
            elif mode == "loadonly" or mode == "dmaonly":
                ve0 = p.sem("ve")
                for c in range(NC):
                    p.vE.tensor_copy(out=xnb[:, c * T:(c + 1) * T],
                                     in_=x_fm[:, c * T:(c + 1) * T])
            p.join()
            dump = lw["xn32"]
            ve, so = p.sem("ve"), p.sem("so")
            for c in range(NC):
                dsrc = x_fm if mode == "rt" else xnb
                p.vE.tensor_copy(out=dump, in_=dsrc[:, c * T:(c + 1) * T])
                p.yE.wait_ge(ve.sem, ve.v)
                p.yE.dma_start(out=xn_out[c * 128:(c + 1) * 128, :], in_=dump).then_inc(so.sem, 16)
                so.inc(16)
                p.vE.wait_ge(so.sem, so.v)
                if mode == "full":
                    p.vE.tensor_copy(out=dump, in_=xq[:, c * T:(c + 1) * T])
                    p.yE.wait_ge(ve.sem, ve.v)
                    p.yE.dma_start(out=xq_out[c * 128:(c + 1) * 128, :], in_=dump).then_inc(so.sem, 16)
                    so.inc(16)
                    p.vE.wait_ge(so.sem, so.v)
            p.yE.wait_ge(so.sem, so.v)
    return p.nc


if __name__ == "__main__":
    import sys
    sys.path.insert(0, "/root/problem/scratch")
    import golden_chain as G
    import ml_dtypes

    rng = np.random.default_rng(0)
    x = (rng.standard_normal((T, D)) * (1.0 + rng.standard_normal((T, 1)) * 0.3)).astype(np.float32)
    import sys as _s
    nc = build_test_ln(_s.argv[1] if len(_s.argv) > 1 else "full")
    res = run_bass_kernel_spmd(nc, [{"x": x}], core_ids=[0])
    out = res.results[0]
    xn_dev = out["xn_out"].reshape(NC, 128, T).transpose(2, 0, 1).reshape(T, D)
    xq_dev = out["xq_out"].reshape(NC, 128, T).transpose(2, 0, 1).reshape(T, D)
    xn_ref = G.layernorm(x)
    xnb_ref = xn_ref.astype(ml_dtypes.bfloat16).astype(np.float32)
    xq_ref = G.fp4q(xn_ref).astype(ml_dtypes.bfloat16).astype(np.float32)
    print("xn maxerr:", np.abs(xn_dev - xnb_ref).max(),
          " rel:", np.abs(xn_dev - xnb_ref).max() / np.abs(xn_ref).max())
    print("xq maxerr:", np.abs(xq_dev - xq_ref).max(),
          " mismatches:", (xq_dev != xq_ref).sum(), "/", xq_ref.size)
    num = ((xq_dev - xq_ref) ** 2).mean(); den = (xq_ref ** 2).mean()
    print("xq res-var ratio:", num / den)
    big = np.abs(xq_dev - xq_ref) > 0.2 * np.abs(xq_ref).mean()
    print("fraction grid-step-flips:", big.mean())



def park_tiles(p, src_sb, dst_dram, r0, engine="yE"):
    so = p.sem("so")
    getattr(p, engine).dma_start(out=dst_dram[r0:r0 + 128, :src_sb.shape[1]],
                                 in_=src_sb).then_inc(so.sem, 16)
    so.inc(16)



def emit_l01_weight_prepass(p, win, scratch, qw):
    qg, so, ve = p.sem("qg"), p.sem("ve"), p.sem("so")
    with p.scope():
        stg = p.sb("wp_stage", [128, 2 * 2816], BF16)
        for srcs, dst in [((win["w_q"], win["w_k"], win["w_v"]), scratch["wqkv_bf"]),
                          ((win["w_o"],), scratch["wo_bf"])]:
            r0 = 0
            for srcw in srcs:
                R, C = srcw.shape
                for i in range(R // 128):
                    buf = stg[:, (i % 2) * 2816:(i % 2) * 2816 + C]
                    p.gE.wait_ge(so.sem, max(0, so.v - 16))
                    p.gE.dma_start(out=buf, in_=srcw[i * 128:(i + 1) * 128, :]).then_inc(qg.sem, 16)
                    qg.inc(16)
                    p.yE.wait_ge(qg.sem, qg.v)
                    p.yE.dma_start(out=dst[r0 + i * 128:r0 + (i + 1) * 128, :],
                                   in_=buf).then_inc(so.sem, 16)
                    so.inc(16)
                r0 += R
        f32stg = p.sb("wp_f32", [128, 2816], F32)
        bfstg = p.sb("wp_bf", [128, 2816], BF16)
        qs = p.sem("qs")
        for srcw, dst in [(win["w_gate"], scratch["wg_q"]), (win["w_up"], scratch["wu_q"]),
                          (win["w_down"], scratch["wd_q"])]:
            R, C = srcw.shape
            for i in range(R // 128):
                p.yE.wait_ge(p.sem("ve").sem, p.sem("ve").v)
                p.yE.dma_start(out=f32stg[:, 0:C], in_=srcw[i * 128:(i + 1) * 128, :]).then_inc(qs.sem, 16)
                qs.inc(16)
                p.vE.wait_ge(qs.sem, qs.v)
                p.vE.wait_ge(so.sem, so.v)
                if C <= 2048:
                    emit_fp4_quant(p, f32stg[:, 0:C], bfstg[:, 0:C], qw, chan_transpose=False)
                else:
                    emit_fp4_quant(p, f32stg[:, 0:1408], bfstg[:, 0:1408], qw, chan_transpose=False)
                    emit_fp4_quant(p, f32stg[:, 1408:2816], bfstg[:, 1408:2816], qw, chan_transpose=False)
                p.yE.wait_ge(p.sem("ve").sem, p.sem("ve").v)
                p.yE.dma_start(out=dst[i * 128:(i + 1) * 128, :], in_=bfstg[:, 0:C]).then_inc(so.sem, 16)
                so.inc(16)


FG_SIZES = [512, 512, 512, 512, 512, 256]
FG_OFF = [0, 512, 1024, 1536, 2048, 2560]


def emit_ffn_fp4(p, xq_dram, g_dram, scratch, qw, dbg_hq=None, dbg_h=None):
    qs, qg, mm, ve, ac, so = (p.sem(k) for k in ["qs", "qg", "mm", "ve", "ac", "so"])
    wg_q, wu_q, wd_q = scratch["wg_q"], scratch["wu_q"], scratch["wd_q"]
    with p.scope():
        xq_sb = p.sb("ffn_xq", [128, NC * 512], BF16)
        hq_sb = p.sb("ffn_hq", [128, 22 * 512], BF16)
        hsil = p.sb("ffn_hsil", [128, 512], F32)
        h32 = p.sb("ffn_h32", [128, 512], F32)
        gstg = p.sb("ffn_gstg", [128, 512], BF16)
        ps_g = p.ps("ffn_psg", [128, 512])
        ps_u = p.ps("ffn_psu", [128, 512])
        ps_d = p.ps("ffn_psd", [128, 512])
        for qr in range(4):
            t0 = qr * 512
            p.yE.wait_ge(mm.sem, mm.v)
            p.vE.wait_ge(mm.sem, mm.v)
            for c in range(NC):
                p.yE.dma_start(out=xq_sb[:, c * 512:(c + 1) * 512],
                               in_=xq_dram[c * 128:(c + 1) * 128, t0:t0 + 512]).then_inc(qs.sem, 16)
                qs.inc(16)
            with p.scope():
                wgt = p.sb("ffn_wgt", [128, NC * 512], BF16)
                wut = p.sb("ffn_wut", [128, NC * 512], BF16)
                for fg in range(6):
                    fsz, foff = FG_SIZES[fg], FG_OFF[fg]
                    p.yE.wait_ge(mm.sem, mm.v)
                    for c in range(NC):
                        p.yE.dma_start_transpose(
                            out=wgt[:, c * 512:c * 512 + fsz],
                            in_=wg_q[foff:foff + fsz, c * 128:(c + 1) * 128]).then_inc(qs.sem, 16)
                        qs.inc(16)
                        p.yE.dma_start_transpose(
                            out=wut[:, c * 512:c * 512 + fsz],
                            in_=wu_q[foff:foff + fsz, c * 128:(c + 1) * 128]).then_inc(qs.sem, 16)
                        qs.inc(16)
                    p.tE.wait_ge(qs.sem, qs.v)
                    for ft in range(fsz // 128):
                        fgl = (foff + ft * 128) // 128
                        p.tE.wait_ge(ve.sem, ve.v)
                        for c in range(NC):
                            p.tE.matmul(out=ps_g[:], lhsT=wgt[:, c * 512 + ft * 128:c * 512 + ft * 128 + 128],
                                        rhs=xq_sb[:, c * 512:(c + 1) * 512],
                                        start=(c == 0), stop=(c == NC - 1))
                        for c in range(NC):
                            mi = p.tE.matmul(out=ps_u[:], lhsT=wut[:, c * 512 + ft * 128:c * 512 + ft * 128 + 128],
                                             rhs=xq_sb[:, c * 512:(c + 1) * 512],
                                             start=(c == 0), stop=(c == NC - 1))
                        mi.then_inc(mm.sem, 1)
                        mm.inc(1)
                        p.sE.wait_ge(mm.sem, mm.v)
                        p.sE.activation(out=hsil[:], in_=ps_g[:], func=AF.Silu)
                        p.vE.wait_ge(ac.sem, ac.v)
                        p.vE.tensor_tensor(out=h32[:], in0=ps_u[:], in1=hsil[:], op=ALU.mult)
                        if dbg_h is not None:
                            p.yE.wait_ge(ve.sem, ve.v)
                            p.yE.dma_start(out=dbg_h[fgl * 128:(fgl + 1) * 128, t0:t0 + 512],
                                           in_=h32[:]).then_inc(so.sem, 16)
                            so.inc(16)
                        emit_fp4_quant(p, h32[:], hq_sb[:, fgl * 512:(fgl + 1) * 512], qw,
                                       chan_transpose=True)
                        if dbg_hq is not None:
                            p.yE.wait_ge(ve.sem, ve.v)
                            p.yE.dma_start(out=dbg_hq[fgl * 128:(fgl + 1) * 128, t0:t0 + 512],
                                           in_=hq_sb[:, fgl * 512:(fgl + 1) * 512]).then_inc(so.sem, 16)
                            so.inc(16)
            with p.scope():
                wdt = p.sb("ffn_wdt", [128, 22 * 512], BF16)
                for dh in range(2):
                    p.yE.wait_ge(mm.sem, mm.v)
                    for fc in range(22):
                        p.yE.dma_start_transpose(
                            out=wdt[:, fc * 512:(fc + 1) * 512],
                            in_=wd_q[dh * 512:(dh + 1) * 512, fc * 128:(fc + 1) * 128]).then_inc(qs.sem, 16)
                        qs.inc(16)
                    p.tE.wait_ge(qs.sem, qs.v)
                    for dt in range(4):
                        dgl = dh * 4 + dt
                        p.tE.wait_ge(so.sem, so.v)
                        for fc in range(22):
                            mi = p.tE.matmul(out=ps_d[:], lhsT=wdt[:, fc * 512 + dt * 128:fc * 512 + dt * 128 + 128],
                                             rhs=hq_sb[:, fc * 512:(fc + 1) * 512],
                                             start=(fc == 0), stop=(fc == 21))
                        mi.then_inc(mm.sem, 1)
                        mm.inc(1)
                        p.sE.wait_ge(mm.sem, mm.v)
                        p.sE.activation(out=gstg[:], in_=ps_d[:], func=AF.Silu)
                        p.yE.wait_ge(ac.sem, ac.v)
                        p.yE.dma_start(out=g_dram[dgl * 128:(dgl + 1) * 128, t0:t0 + 512],
                                       in_=gstg[:]).then_inc(so.sem, 16)
                        so.inc(16)



def emit_l0_attn(p, xnb_dram, scratch, g_dram, x_fm):
    qs, qg, mm, ve, ac, so = (p.sem(k) for k in ["qs", "qg", "mm", "ve", "ac", "so"])
    wqkv_bf, wo_bf = scratch["wqkv_bf"], scratch["wo_bf"]
    q_fm = p.sb("at_q", [128, 8 * T], BF16)
    k_fm = p.sb("at_k", [128, 8 * T], BF16)
    v_tm = p.sb("at_v", [128, NT * 16 * 65], BF16)
    with p.scope():
        xnb_sb = p.sb("at_xnb", [128, NC * 1024], BF16)
        wt_sb = p.sb("at_wt", [128, NC * 512], BF16)
        ps_a = p.ps("at_psa", [128, 512])
        ps_b2 = p.ps("at_psb", [128, 512])
        p.gE.memset(v_tm[:].rearrange("p (t h o) -> p (t h) o", h=16, o=65)[:, :, 64:65], 1.0)
        for th in range(2):
            tb = th * 1024
            p.yE.wait_ge(mm.sem, mm.v)
            for c in range(NC):
                p.yE.dma_start(out=xnb_sb[:, c * 1024:(c + 1) * 1024],
                               in_=xnb_dram[c * 128:(c + 1) * 128, tb:tb + 1024]).then_inc(qs.sem, 16)
                qs.inc(16)
            for tensor_i, dst in [(0, q_fm), (1, k_fm), (2, None)]:
                for hg in range(2):
                    p.yE.wait_ge(mm.sem, mm.v)
                    for c in range(NC):
                        p.yE.dma_start_transpose(
                            out=wt_sb[:, c * 512:(c + 1) * 512],
                            in_=wqkv_bf[tensor_i * 1024 + hg * 512:tensor_i * 1024 + (hg + 1) * 512,
                                        c * 128:(c + 1) * 128]).then_inc(qs.sem, 16)
                        qs.inc(16)
                    p.tE.wait_ge(qs.sem, qs.v)
                    if dst is not None:
                        for ht in range(4):
                            hd_tile = hg * 4 + ht
                            for sl in range(2):
                                ps = ps_a if (ht * 2 + sl) % 2 == 0 else ps_b2
                                p.tE.wait_ge(ve.sem, ve.v)
                                for c in range(NC):
                                    mi = p.tE.matmul(
                                        out=ps[:],
                                        lhsT=wt_sb[:, c * 512 + ht * 128:c * 512 + ht * 128 + 128],
                                        rhs=xnb_sb[:, c * 1024 + sl * 512:c * 1024 + (sl + 1) * 512],
                                        start=(c == 0), stop=(c == NC - 1))
                                mi.then_inc(mm.sem, 1)
                                mm.inc(1)
                                p.vE.wait_ge(mm.sem, mm.v)
                                p.vE.tensor_copy(
                                    out=dst[:, hd_tile * T + tb + sl * 512:hd_tile * T + tb + (sl + 1) * 512],
                                    in_=ps[:])
                    else:
                        p.tE.wait_ge(p.sem("po").sem, p.sem("po").v)
                        for tt in range(8):
                            gt = th * 8 + tt
                            ps = ps_a if tt % 2 == 0 else ps_b2
                            p.tE.wait_ge(ve.sem, ve.v)
                            for c in range(NC):
                                mi = p.tE.matmul(
                                    out=ps[:],
                                    lhsT=xnb_sb[:, c * 1024 + tt * 128:c * 1024 + tt * 128 + 128],
                                    rhs=wt_sb[:, c * 512:(c + 1) * 512],
                                    start=(c == 0), stop=(c == NC - 1))
                            mi.then_inc(mm.sem, 1)
                            mm.inc(1)
                            p.vE.wait_ge(mm.sem, mm.v)
                            base = gt * 16 * 65 + (8 * hg) * 65
                            outap = bass.AP(v_tm[:].tensor, v_tm[:].offset + base,
                                            [v_tm[:].ap[0], [65, 8], [1, 64]])
                            p.vE.tensor_copy(out=outap, in_=ps[:].rearrange("p (h o) -> p h o", o=64))
    p.join()
    nm_all = p.sb("at_nm", [128, 256], F32)
    nmT_bf = p.sb("at_nmT", [128, 256], BF16)
    with p.scope():
        dtmp = p.sb("at_dtmp", [128, 128], F32)
        ps_d1 = p.ps("at_psd1", [128, 128])
        ps_d2 = p.ps("at_psd2", [128, 128])
        for h in range(16):
            hp, ho = h // 2, (h % 2) * 64
            for i in range(NT):
                ps = ps_d1 if (h * NT + i) % 2 == 0 else ps_d2
                p.tE.wait_ge(ve.sem, ve.v)
                p.tE.matmul(out=ps[:],
                            lhsT=q_fm[ho:ho + 64, hp * T + i * 128:hp * T + (i + 1) * 128],
                            rhs=k_fm[ho:ho + 64, hp * T + i * 128:hp * T + (i + 1) * 128],
                            start=True, stop=True).then_inc(mm.sem, 1)
                mm.inc(1)
                p.vE.wait_ge(mm.sem, mm.v)
                p.vE.tensor_tensor(out=dtmp[:], in0=ps[:], in1=p.negm[:], op=ALU.add)
                p.vE.reduce_max(out=nm_all[:, h * 16 + i:h * 16 + i + 1], in_=dtmp[:],
                                axis=AX.X, negate=True)
        ps_t = ps_d1
        for half in range(2):
            p.tE.wait_ge(ve.sem, ve.v)
            p.tE.transpose(out=ps_t[:], in_=nm_all[:, half * 128:(half + 1) * 128],
                           identity=p.id_f32[:]).then_inc(mm.sem, 1)
            mm.inc(1)
            p.vE.wait_ge(mm.sem, mm.v)
            p.vE.tensor_copy(out=nmT_bf[:, half * 128:(half + 1) * 128], in_=ps_t[:])
    p.join()
    with p.scope():
        y_sl = p.sb("at_y", [128, 8 * 512], BF16)
        peT = p.sb("at_peT", [128, 2 * 512], BF16)
        shrow = p.sb("at_shrow", [1, 512], BF16)
        recipb = p.sb("at_recipb", [64, 512], F32)
        wo_sb = p.sb("at_wo", [128, 8 * 1024], BF16)
        gbuf = p.sb("at_gbuf", [128, 2 * 512], BF16)
        cmb = p.sb("at_cmb", [128, 512], F32)
        ps_s1 = p.ps("at_pss1", [128, 512])
        ps_s2 = p.ps("at_pss2", [128, 512])
        ps_y = p.ps("at_psy", [65, 512])
        ps_r = p.ps("at_psr", [64, 512])
        ps_o1 = p.ps("at_pso1", [128, 512])
        ps_o2 = p.ps("at_pso2", [128, 512])
        for yc in range(8):
            p.yE.dma_start(out=wo_sb[:, yc * 1024:(yc + 1) * 1024],
                           in_=wo_bf[yc * 128:(yc + 1) * 128, :]).then_inc(qs.sem, 16)
            qs.inc(16)
        exp_ac_bank = [0, 0]
        y_evac_ve = 0
        for tqs in range(4):
            for h in range(16):
                hp, ho = h // 2, (h % 2) * 64
                rowbase = h * 16 + tqs * 4
                tilA = rowbase // 128
                rr = rowbase % 128
                p.yE.wait_ge(mm.sem, mm.v)
                p.yE.dma_start(out=shrow[:],
                               in_=nmT_bf[rr:rr + 4, tilA * 128:(tilA + 1) * 128]).then_inc(qs.sem, 16)
                qs.inc(16)
                ntk = tqs * 4 + 4
                p.tE.wait_ge(qs.sem, qs.v)
                for tkc in range(ntk):
                    off = max(0, tkc - tqs * 4) * 128
                    ps_s = ps_s1 if tkc % 2 == 0 else ps_s2
                    p.tE.wait_ge(ac.sem, exp_ac_bank[tkc % 2])
                    p.tE.matmul(out=ps_s[:, off:512],
                                lhsT=k_fm[ho:ho + 64, hp * T + tkc * 128:hp * T + (tkc + 1) * 128],
                                rhs=q_fm[ho:ho + 64, hp * T + tqs * 512 + off:hp * T + (tqs + 1) * 512],
                                start=True, stop=False)
                    p.tE.matmul(out=ps_s[:, off:512],
                                lhsT=p.ones_p1_bf[:],
                                rhs=shrow[:, off:512],
                                start=False, stop=True).then_inc(mm.sem, 1)
                    mm.inc(1)
                    if tkc >= tqs * 4:
                        p.vE.wait_ge(mm.sem, mm.v)
                        p.vE.tensor_tensor(out=ps_s[:, off:off + 128], in0=ps_s[:, off:off + 128],
                                           in1=p.negmT[:], op=ALU.add)
                        p.sE.wait_ge(ve.sem, ve.v)
                    else:
                        p.sE.wait_ge(mm.sem, mm.v)
                    pe_t = peT[:, (tkc % 2) * 512:(tkc % 2) * 512 + 512]
                    p.sE.activation(out=pe_t[:, off:512], in_=ps_s[:, off:512],
                                    func=AF.Exp, scale=float(1.0 / 8.0))
                    exp_ac_bank[tkc % 2] = ac.v
                    p.tE.wait_ge(ac.sem, ac.v)
                    if tkc == 0:
                        p.tE.wait_ge(ve.sem, y_evac_ve)
                    p.tE.matmul(out=ps_y[:, off:512],
                                lhsT=v_tm[:, tkc * 16 * 65 + h * 65:tkc * 16 * 65 + (h + 1) * 65],
                                rhs=pe_t[:, off:512],
                                start=(tkc == 0), stop=(tkc == ntk - 1)).then_inc(mm.sem, 1)
                    mm.inc(1)
                p.sE.wait_ge(mm.sem, mm.v)
                p.sE.activation(out=recipb[0:1, :], in_=ps_y[64:65, :], func=AF.Ln)
                p.sE.activation(out=recipb[0:1, :], in_=recipb[0:1, :], func=AF.Exp,
                                scale=float(-1.0))
                p.tE.wait_ge(ac.sem, ac.v)
                p.tE.matmul(out=ps_r[:], lhsT=p.ones_p1_f32[:, 0:64], rhs=recipb[0:1, :],
                            start=True, stop=True).then_inc(mm.sem, 1)
                mm.inc(1)
                p.vE.wait_ge(mm.sem, mm.v)
                p.vE.tensor_copy(out=recipb[:], in_=ps_r[:])
                p.vE.tensor_tensor(out=y_sl[ho:ho + 64, hp * 512:(hp + 1) * 512],
                                   in0=ps_y[0:64, :], in1=recipb[:], op=ALU.mult)
                y_evac_ve = ve.v
            for dc in range(NC):
                ps_o = ps_o1 if dc % 2 == 0 else ps_o2
                p.tE.wait_ge(ve.sem, ve.v)
                for yc in range(NC):
                    mi = p.tE.matmul(out=ps_o[:],
                                     lhsT=wo_sb[:, yc * 1024 + dc * 128:yc * 1024 + dc * 128 + 128],
                                     rhs=y_sl[:, yc * 512:(yc + 1) * 512],
                                     start=(yc == 0), stop=(yc == NC - 1))
                mi.then_inc(mm.sem, 1)
                mm.inc(1)
                gb = gbuf[:, (dc % 2) * 512:(dc % 2) * 512 + 512]
                p.gE.wait_ge(ve.sem, ve.v)
                p.gE.dma_start(out=gb, in_=g_dram[dc * 128:(dc + 1) * 128,
                                                  tqs * 512:(tqs + 1) * 512]).then_inc(qg.sem, 16)
                qg.inc(16)
                p.vE.wait_ge(mm.sem, mm.v)
                p.vE.wait_ge(qg.sem, qg.v)
                p.vE.tensor_tensor(out=cmb[:], in0=ps_o[:], in1=gb, op=ALU.mult)
                xsl = x_fm[:, dc * T + tqs * 512:dc * T + (tqs + 1) * 512]
                p.vE.tensor_tensor(out=xsl, in0=xsl, in1=cmb[:], op=ALU.add)



def build_layer0(dbg=False):
    p = Prog()
    x_in = p.dram("x", [T, D], F32, kind="ExternalInput")
    w_q = p.dram("w_q", [1024, 1024], F32, kind="ExternalInput")
    w_k = p.dram("w_k", [1024, 1024], F32, kind="ExternalInput")
    w_v = p.dram("w_v", [1024, 1024], F32, kind="ExternalInput")
    w_o = p.dram("w_o", [1024, 1024], F32, kind="ExternalInput")
    w_gate = p.dram("w_gate", [2816, 1024], F32, kind="ExternalInput")
    w_up = p.dram("w_up", [2816, 1024], F32, kind="ExternalInput")
    w_down = p.dram("w_down", [1024, 2816], F32, kind="ExternalInput")
    x_out = p.dram("x_out", [T, D], F32, kind="ExternalOutput")

    scratch = {
        "wqkv_bf": p.dram("wqkv_bf", [3072, 1024], BF16),
        "wo_bf": p.dram("wo_bf", [1024, 1024], BF16),
        "wg_q": p.dram("wg_q", [2816, 1024], BF16),
        "wu_q": p.dram("wu_q", [2816, 1024], BF16),
        "wd_q": p.dram("wd_q", [1024, 2816], BF16),
    }
    kind_dbg = "ExternalOutput" if dbg else "Internal"
    xnb_dram = p.dram("xnb_park", [1024, T], BF16, kind=kind_dbg)
    xq_dram = p.dram("xq_park", [1024, T], BF16, kind=kind_dbg)
    g_dram = p.dram("g_park", [1024, T], BF16, kind=kind_dbg)
    win = {"w_q": w_q, "w_k": w_k, "w_v": w_v, "w_o": w_o,
           "w_gate": w_gate, "w_up": w_up, "w_down": w_down}

    x_fm = p.sb("x_fm", [128, NC * T], F32)
    blk = p.es.enter_context(p.nc.Block())
    with p.es:
        @blk.sync
        def _(e):
            p.consts()
            with p.scope():
                qw = make_quant_work(p)
                emit_l01_weight_prepass(p, win, scratch, qw)
                with p.scope():
                    stage = p.sb("stage", [128, 2 * D], F32)
                    pb = p.ps("pb", [128, 1024])
                    p.tE.wait_ge(p.sem("po").sem, p.sem("po").v)
                    emit_load_x_fm(p, x_in, x_fm[:], stage[:], pb[:])
                p.join()
                with p.scope():
                    lw = make_ln_work(p)
                    lw.update(qw)
                    xnb_c = p.sb("l0_xnbc", [128, T], BF16)
                    xq_c = p.sb("l0_xqc", [128, T], BF16)
                    emit_ln_parked(p, x_fm[:], xnb_c[:], xq_c[:], lw, xnb_dram, xq_dram)
                p.join()
                dbg_hq = p.dram("hq_park", [2816, T], BF16, kind="ExternalOutput") if dbg else None
                dbg_h = p.dram("h_park", [2816, T], F32, kind="ExternalOutput") if dbg else None
                emit_ffn_fp4(p, xq_dram, g_dram, scratch, qw, dbg_hq=dbg_hq, dbg_h=dbg_h)
                p.join()
            with p.scope():
                emit_l0_attn(p, xnb_dram, scratch, g_dram, x_fm[:])
            p.join()
            with p.scope():
                stage = p.sb("stage2", [128, 2 * D], F32)
                pb = p.ps("pb2", [128, 1024])
                emit_store_fm(p, x_fm[:], x_out, stage[:], pb[:])
            p.join()
    return p.nc


def emit_ln_parked(p, x_fm, xnb_c, xq_c, w, xnb_dram, xq_dram, router=None):
    so, ve = p.sem("so"), p.sem("ve")
    mm, ac = p.sem("mm"), p.sem("ac")
    xb, sq = w["xb"], w["sq"]
    ps_sx, ps_sxx = w["ps_sx"], w["ps_sxx"]
    sxr, sxxr = w["sxr"], w["sxxr"]
    tmpr = w["rstd_b"][0:1, 0:T]
    ps_b = w["ps_b"]
    rstd_b, mr_b = w["rstd_b"], w["mr_b"]
    xn32 = w["xn32"]
    p.vE.memset(sxr, 0.0)
    p.vE.memset(sxxr, 0.0)
    for c in range(NC):
        if c >= 1:
            p.vE.wait_ge(mm.sem, mm.v)
        p.vE.tensor_copy(out=xb, in_=x_fm[:, c * T:(c + 1) * T])
        p.sE.wait_ge(ve.sem, ve.v)
        p.sE.activation(out=sq, in_=xb, func=AF.Square)
        p.tE.wait_ge(ac.sem, ac.v)
        for sl in range(4):
            p.tE.wait_ge(ve.sem, ve.v)
            p.tE.matmul(out=ps_sx, lhsT=p.ones_col_bf[:], rhs=xb[:, sl * 512:(sl + 1) * 512],
                        start=True, stop=True)
            p.tE.matmul(out=ps_sxx, lhsT=p.ones_col_bf[:], rhs=sq[:, sl * 512:(sl + 1) * 512],
                        start=True, stop=True).then_inc(mm.sem, 1)
            mm.inc(1)
            p.vE.wait_ge(mm.sem, mm.v)
            p.vE.tensor_tensor(out=sxr[:, sl * 512:(sl + 1) * 512],
                               in0=sxr[:, sl * 512:(sl + 1) * 512], in1=ps_sx, op=ALU.add)
            p.vE.tensor_tensor(out=sxxr[:, sl * 512:(sl + 1) * 512],
                               in0=sxxr[:, sl * 512:(sl + 1) * 512], in1=ps_sxx,
                               op=ALU.add)
    p.vE.tensor_scalar(out=sxr, in0=sxr, scalar1=float(1.0 / D), scalar2=None, op0=ALU.mult)
    p.vE.tensor_scalar(out=sxxr, in0=sxxr, scalar1=float(1.0 / D), scalar2=None, op0=ALU.mult)
    p.vE.tensor_tensor(out=tmpr, in0=sxr, in1=sxr, op=ALU.mult)
    p.vE.tensor_tensor(out=sxxr, in0=sxxr, in1=tmpr, op=ALU.subtract)
    p.sE.wait_ge(ve.sem, ve.v)
    p.sE.activation(out=sxxr, in_=sxxr, func=AF.Ln, bias=p.cf['eps5'][0:1, :])
    p.sE.activation(out=sxxr, in_=sxxr, func=AF.Exp, scale=float(-0.5))
    p.vE.wait_ge(ac.sem, ac.v)
    p.vE.tensor_tensor(out=sxr, in0=sxr, in1=sxxr, op=ALU.mult)
    p.tE.wait_ge(ve.sem, ve.v)
    for src_row, dst in [(sxxr, rstd_b), (sxr, mr_b)]:
        for sl in range(4):
            p.tE.matmul(out=ps_b, lhsT=p.ones_p1_f32[:],
                        rhs=src_row[:, sl * 512:(sl + 1) * 512],
                        start=True, stop=True).then_inc(mm.sem, 1)
            mm.inc(1)
            p.vE.wait_ge(mm.sem, mm.v)
            p.vE.tensor_copy(out=dst[:, sl * 512:(sl + 1) * 512], in_=ps_b)
            p.tE.wait_ge(ve.sem, ve.v)
    for c in range(NC):
        p.vE.wait_ge(so.sem, so.v)
        if router is not None:
            p.vE.wait_ge(mm.sem, mm.v)
        p.vE.tensor_tensor(out=xn32, in0=x_fm[:, c * T:(c + 1) * T], in1=rstd_b, op=ALU.mult)
        p.vE.tensor_tensor(out=xn32, in0=xn32, in1=mr_b, op=ALU.subtract)
        p.vE.tensor_copy(out=xnb_c, in_=xn32)
        if router is not None:
            rw_sb, lg_ps = router
            p.tE.wait_ge(ve.sem, ve.v)
            for sl in range(4):
                mi = p.tE.matmul(out=lg_ps[:, sl * 512:(sl + 1) * 512],
                                 lhsT=rw_sb[:, c * 8:(c + 1) * 8],
                                 rhs=xn32[:, sl * 512:(sl + 1) * 512],
                                 start=(c == 0), stop=(c == NC - 1))
            mi.then_inc(mm.sem, 1)
            mm.inc(1)
        if xq_c is not None:
            emit_fp4_quant(p, xn32, xq_c, w)
        p.yE.wait_ge(ve.sem, ve.v)
        p.yE.dma_start(out=xnb_dram[c * 128:(c + 1) * 128, :], in_=xnb_c).then_inc(so.sem, 16)
        so.inc(16)
        if xq_c is not None:
            p.yE.dma_start(out=xq_dram[c * 128:(c + 1) * 128, :], in_=xq_c).then_inc(so.sem, 16)
            so.inc(16)



def emit_l1_attn(p, xnb_dram, scratch, g_dram, x_fm, use_shift=True, dbg_pe=None, s_out=None):
    qs, qg, mm, ve, ac, so = (p.sem(k) for k in ["qs", "qg", "mm", "ve", "ac", "so"])
    qk_bf, ov_bf = scratch["qk_bf"], scratch["ov_bf"]
    y_fm = p.sb("l1_y", [128, 8 * T], BF16)
    w_tm = p.sb("l1_w", [128, NT * 1025], BF16)
    xnb_sb = p.sb("l1_xnb", [128, NC * T], BF16)
    for c in range(NC):
        p.yE.dma_start(out=xnb_sb[:, c * T:(c + 1) * T],
                       in_=xnb_dram[c * 128:(c + 1) * 128, :]).then_inc(qs.sem, 16)
        qs.inc(16)
    p.gE.memset(w_tm[:].rearrange("p (t o) -> p t o", o=1025)[:, :, 1024:1025], 1.0)
    with p.scope():
        wsb = p.sb("l1_wsb", [128, NC * 1024], BF16)
        ps_a = p.ps("l1_psa", [128, 512])
        ps_b2 = p.ps("l1_psb", [128, 512])
        for which, (src, dst_mode) in enumerate([(qk_bf, "y"), (ov_bf, "w")]):
            p.yE.wait_ge(mm.sem, mm.v)
            for c in range(NC):
                p.yE.dma_start(out=wsb[:, c * 1024:(c + 1) * 1024],
                               in_=src[c * 128:(c + 1) * 128, :]).then_inc(qs.sem, 16)
                qs.inc(16)
            p.tE.wait_ge(qs.sem, qs.v)
            if dst_mode == "y":
                for et in range(8):
                    for sl in range(4):
                        ps = ps_a if (et * 4 + sl) % 2 == 0 else ps_b2
                        p.tE.wait_ge(ve.sem, ve.v)
                        for c in range(NC):
                            mi = p.tE.matmul(
                                out=ps[:],
                                lhsT=wsb[:, c * 1024 + et * 128:c * 1024 + et * 128 + 128],
                                rhs=xnb_sb[:, c * T + sl * 512:c * T + (sl + 1) * 512],
                                start=(c == 0), stop=(c == NC - 1))
                        mi.then_inc(mm.sem, 1)
                        mm.inc(1)
                        p.vE.wait_ge(mm.sem, mm.v)
                        p.vE.tensor_copy(out=y_fm[:, et * T + sl * 512:et * T + (sl + 1) * 512],
                                         in_=ps[:])
            else:
                p.tE.wait_ge(p.sem("po").sem, p.sem("po").v)
                for tt in range(NT):
                    for eg in range(2):
                        ps = ps_a if (tt * 2 + eg) % 2 == 0 else ps_b2
                        p.tE.wait_ge(ve.sem, ve.v)
                        for c in range(NC):
                            mi = p.tE.matmul(
                                out=ps[:],
                                lhsT=xnb_sb[:, c * T + tt * 128:c * T + tt * 128 + 128],
                                rhs=wsb[:, c * 1024 + eg * 512:c * 1024 + (eg + 1) * 512],
                                start=(c == 0), stop=(c == NC - 1))
                        mi.then_inc(mm.sem, 1)
                        mm.inc(1)
                        p.vE.wait_ge(mm.sem, mm.v)
                        p.vE.tensor_copy(
                            out=w_tm[:, tt * 1025 + eg * 512:tt * 1025 + (eg + 1) * 512],
                            in_=ps[:])
    p.join()
    if dbg_pe is not None and len(dbg_pe) > 2:
        so2 = p.sem("so")
        p.yE.dma_start(out=dbg_pe[2], in_=y_fm[:]).then_inc(so2.sem, 16)
        so2.inc(16)
        p.yE.dma_start(out=dbg_pe[3], in_=w_tm[:]).then_inc(so2.sem, 16)
        so2.inc(16)
        if len(dbg_pe) > 4 and dbg_pe[4] == "projonly":
            p.yE.wait_ge(so2.sem, so2.v)
            return
    nm_all = p.sb("l1_nm", [128, 16], F32)
    nmT_bf = p.sb("l1_nmT", [128, 128], BF16)
    with p.scope():
        dtmp = p.sb("l1_dtmp", [128, 128], F32)
        ps_d1 = p.ps("l1_psd1", [128, 128])
        ps_d2 = p.ps("l1_psd2", [128, 128])
        for i in range(NT):
            ps = ps_d1 if i % 2 == 0 else ps_d2
            p.tE.wait_ge(ve.sem, ve.v)
            for c in range(NC):
                mi = p.tE.matmul(out=ps[:],
                                 lhsT=y_fm[:, c * T + i * 128:c * T + (i + 1) * 128],
                                 rhs=xnb_sb[:, c * T + i * 128:c * T + (i + 1) * 128],
                                 start=(c == 0), stop=(c == NC - 1))
            mi.then_inc(mm.sem, 1)
            mm.inc(1)
            p.vE.wait_ge(mm.sem, mm.v)
            p.vE.tensor_tensor(out=dtmp[:], in0=ps[:], in1=p.negm[:], op=ALU.add)
            p.vE.reduce_max(out=nm_all[:, i:i + 1], in_=dtmp[:], axis=AX.X, negate=True)
        p.tE.wait_ge(ve.sem, ve.v)
        p.tE.transpose(out=ps_d1[0:16, :], in_=nm_all[:],
                       identity=p.id_f32[:]).then_inc(mm.sem, 1)
        mm.inc(1)
        p.vE.wait_ge(mm.sem, mm.v)
        p.vE.tensor_copy(out=nmT_bf[0:16, :], in_=ps_d1[0:16, :])
    p.join()
    with p.scope():
        peT = p.sb("l1_peT", [128, 16 * 512], BF16)
        shrow = p.sb("l1_shrow", [1, 512], BF16)
        rb = p.sb("l1_rb", [128, 512], F32)
        srow = p.sb("l1_srow", [1, 512], F32)
        gbuf = p.sb("l1_gbuf", [128, 2 * 512], BF16)
        cmb = p.sb("l1_cmb", [128, 512], F32)
        ps_s1 = p.ps("l1_pss1", [128, 512])
        ps_s2 = p.ps("l1_pss2", [128, 512])
        ps_sum = p.ps("l1_pssum", [1, 512])
        ps_z1 = p.ps("l1_psz1", [128, 512])
        ps_z2 = p.ps("l1_psz2", [128, 512])
        ps_r = p.ps("l1_psr", [128, 512])
        if not use_shift:
            p.gE.memset(shrow[:], 0.0)
            p.tE.wait_ge(p.sem("po").sem, p.sem("po").v)
        exp_ac_bank = [0, 0]
        dbg_ve_bank = [0, 0]
        for tqs in range(4):
            ntk = tqs * 4 + 4
            if use_shift:
                p.yE.wait_ge(mm.sem, mm.v)
                p.yE.dma_start(out=shrow[:],
                               in_=nmT_bf[tqs * 4:tqs * 4 + 4, 0:128]).then_inc(qs.sem, 16)
                qs.inc(16)
                p.tE.wait_ge(qs.sem, qs.v)
            for tkc in range(ntk):
                off = max(0, tkc - tqs * 4) * 128
                ps_s = ps_s1 if tkc % 2 == 0 else ps_s2
                p.tE.wait_ge(ac.sem, exp_ac_bank[tkc % 2])
                if dbg_pe is not None and len(dbg_pe) == 2:
                    p.tE.wait_ge(p.sem("ve").sem, dbg_ve_bank[tkc % 2])
                if use_shift != "none":
                    for c in range(NC):
                        p.tE.matmul(out=ps_s[:, off:512],
                                    lhsT=xnb_sb[:, c * T + tkc * 128:c * T + (tkc + 1) * 128],
                                    rhs=y_fm[:, c * T + tqs * 512 + off:c * T + (tqs + 1) * 512],
                                    start=(c == 0), stop=False)
                    p.tE.matmul(out=ps_s[:, off:512],
                                lhsT=p.ones_p1_bf[:],
                                rhs=shrow[:, off:512],
                                start=False, stop=True).then_inc(mm.sem, 1)
                    mm.inc(1)
                else:
                    for c in range(NC):
                        mi = p.tE.matmul(out=ps_s[:, off:512],
                                         lhsT=xnb_sb[:, c * T + tkc * 128:c * T + (tkc + 1) * 128],
                                         rhs=y_fm[:, c * T + tqs * 512 + off:c * T + (tqs + 1) * 512],
                                         start=(c == 0), stop=(c == NC - 1))
                    mi.then_inc(mm.sem, 1)
                    mm.inc(1)
                if tkc >= tqs * 4:
                    p.vE.wait_ge(mm.sem, mm.v)
                    p.vE.tensor_tensor(out=ps_s[:, off:off + 128], in0=ps_s[:, off:off + 128],
                                       in1=p.negmT[:], op=ALU.add)
                    p.sE.wait_ge(ve.sem, ve.v)
                else:
                    p.sE.wait_ge(mm.sem, mm.v)
                pe_t = peT[:, tkc * 512:(tkc + 1) * 512]
                p.sE.activation(out=pe_t[:, off:512], in_=ps_s[:, off:512],
                                func=AF.Exp, scale=float(1.0 / 32.0))
                exp_ac_bank[tkc % 2] = p.sem("ac").v
                if SERIALIZE_EXP:
                    p.tE.wait_ge(p.sem("ac").sem, p.sem("ac").v)
                if dbg_pe is not None and len(dbg_pe) == 2:
                    r0 = (tqs * (tqs + 1) * 2 + tkc) * 128
                    dbg_t = dbg_pe[1]
                    p.vE.wait_ge(mm.sem, mm.v)
                    p.vE.tensor_copy(out=dbg_t[:, off:512], in_=ps_s[:, off:512])
                    dbg_ve_bank[tkc % 2] = p.sem("ve").v
                    p.yE.wait_ge(p.sem("ve").sem, p.sem("ve").v)
                    p.yE.dma_start(out=dbg_pe[0][r0:r0 + 128, off:512],
                                   in_=dbg_t[:, off:512]).then_inc(so.sem, 16)
                    so.inc(16)
                    p.vE.wait_ge(so.sem, so.v)
            p.tE.wait_ge(ac.sem, ac.v)
            for tkc in range(ntk):
                off = max(0, tkc - tqs * 4) * 128
                mi = p.tE.matmul(out=ps_sum[:, off:512],
                                 lhsT=p.ones_col_bf[:],
                                 rhs=peT[:, tkc * 512 + off:tkc * 512 + 512],
                                 start=(tkc == 0), stop=(tkc == ntk - 1))
            mi.then_inc(mm.sem, 1)
            mm.inc(1)
            p.sE.wait_ge(mm.sem, mm.v)
            p.sE.activation(out=srow[:], in_=ps_sum[:], func=AF.Ln)
            p.sE.activation(out=srow[:], in_=srow[:], func=AF.Exp, scale=float(-1.0))
            p.tE.wait_ge(ac.sem, ac.v)
            p.tE.matmul(out=ps_r[:], lhsT=p.ones_p1_f32[:], rhs=srow[:],
                        start=True, stop=True).then_inc(mm.sem, 1)
            mm.inc(1)
            p.vE.wait_ge(mm.sem, mm.v)
            p.vE.tensor_copy(out=rb[:], in_=ps_r[:])
            for et in range(8):
                ps_z = ps_z1 if et % 2 == 0 else ps_z2
                p.tE.wait_ge(ve.sem, ve.v)
                for tkc in range(ntk):
                    off = max(0, tkc - tqs * 4) * 128
                    mi = p.tE.matmul(out=ps_z[:, off:512],
                                     lhsT=w_tm[:, tkc * 1025 + et * 128:tkc * 1025 + et * 128 + 128],
                                     rhs=peT[:, tkc * 512 + off:tkc * 512 + 512],
                                     start=(tkc == 0), stop=(tkc == ntk - 1))
                mi.then_inc(mm.sem, 1)
                mm.inc(1)
                if s_out is None:
                    gb = gbuf[:, (et % 2) * 512:(et % 2) * 512 + 512]
                    p.gE.wait_ge(ve.sem, ve.v)
                    p.gE.dma_start(out=gb, in_=g_dram[et * 128:(et + 1) * 128,
                                                      tqs * 512:(tqs + 1) * 512]).then_inc(qg.sem, 16)
                    qg.inc(16)
                    p.vE.wait_ge(mm.sem, mm.v)
                    p.vE.wait_ge(qg.sem, qg.v)
                    p.vE.tensor_tensor(out=cmb[:], in0=ps_z[:], in1=rb[:], op=ALU.mult)
                    p.vE.tensor_tensor(out=cmb[:], in0=cmb[:], in1=gb, op=ALU.mult)
                    xsl = x_fm[:, et * T + tqs * 512:et * T + (tqs + 1) * 512]
                    p.vE.tensor_tensor(out=xsl, in0=xsl, in1=cmb[:], op=ALU.add)
                else:
                    p.vE.wait_ge(mm.sem, mm.v)
                    p.vE.wait_ge(so.sem, so.v)
                    p.vE.tensor_tensor(out=cmb[:], in0=ps_z[:], in1=rb[:], op=ALU.mult)
                    p.yE.wait_ge(p.sem("ve").sem, p.sem("ve").v)
                    p.yE.dma_start(out=s_out[et * 128:(et + 1) * 128,
                                             tqs * 512:(tqs + 1) * 512],
                                   in_=cmb[:]).then_inc(so.sem, 16)
                    so.inc(16)


def build_layer1(dbg=False, use_shift=True):
    p = Prog()
    x_in = p.dram("x", [T, D], F32, kind="ExternalInput")
    qk = p.dram("qk", [1024, 1024], F32, kind="ExternalInput")
    ov = p.dram("ov", [1024, 1024], F32, kind="ExternalInput")
    w_gate = p.dram("w_gate", [2816, 1024], F32, kind="ExternalInput")
    w_up = p.dram("w_up", [2816, 1024], F32, kind="ExternalInput")
    w_down = p.dram("w_down", [1024, 2816], F32, kind="ExternalInput")
    x_out = p.dram("x_out", [T, D], F32, kind="ExternalOutput")
    scratch = {
        "qk_bf": p.dram("qk_bf", [1024, 1024], BF16),
        "ov_bf": p.dram("ov_bf", [1024, 1024], BF16),
        "wg_q": p.dram("wg_q", [2816, 1024], BF16),
        "wu_q": p.dram("wu_q", [2816, 1024], BF16),
        "wd_q": p.dram("wd_q", [1024, 2816], BF16),
    }
    kind_dbg = "ExternalOutput" if dbg else "Internal"
    xnb_dram = p.dram("xnb_park", [1024, T], BF16, kind=kind_dbg)
    xq_dram = p.dram("xq_park", [1024, T], BF16, kind=kind_dbg)
    g_dram = p.dram("g_park", [1024, T], BF16, kind=kind_dbg)
    x_fm = p.sb("x_fm", [128, NC * T], F32)
    blk = p.es.enter_context(p.nc.Block())
    with p.es:
        @blk.sync
        def _(e):
            p.consts()
            with p.scope():
                qw = make_quant_work(p)
                win = {"w_q": qk, "w_k": ov, "w_gate": w_gate, "w_up": w_up,
                       "w_down": w_down}
                sc2 = {"wqkv_bf": None, "qk_bf": scratch["qk_bf"],
                       "ov_bf": scratch["ov_bf"], "wg_q": scratch["wg_q"],
                       "wu_q": scratch["wu_q"], "wd_q": scratch["wd_q"]}
                emit_l1_weight_prepass(p, win, sc2, qw)
                with p.scope():
                    stage = p.sb("stage", [128, 2 * D], F32)
                    pb = p.ps("pb", [128, 1024])
                    p.tE.wait_ge(p.sem("po").sem, p.sem("po").v)
                    emit_load_x_fm(p, x_in, x_fm[:], stage[:], pb[:])
                p.join()
                with p.scope():
                    lw = make_ln_work(p)
                    lw.update(qw)
                    xnb_c = p.sb("l1_xnbc", [128, T], BF16)
                    xq_c = p.sb("l1_xqc", [128, T], BF16)
                    emit_ln_parked(p, x_fm[:], xnb_c[:], xq_c[:], lw, xnb_dram, xq_dram)
                p.join()
                emit_ffn_fp4(p, xq_dram, g_dram, scratch, qw)
                p.join()
            with p.scope():
                if dbg:
                    dbg_pe = (p.dram("pe_park", [40 * 128, 512], F32, kind="ExternalOutput"),
                              p.sb("dbg_t", [128, 512], F32)[:])
                else:
                    dbg_pe = None
                emit_l1_attn(p, xnb_dram, scratch, g_dram, x_fm[:], use_shift=use_shift,
                             dbg_pe=dbg_pe)
            p.join()
            with p.scope():
                stage = p.sb("stage2", [128, 2 * D], F32)
                pb = p.ps("pb2", [128, 1024])
                emit_store_fm(p, x_fm[:], x_out, stage[:], pb[:])
            p.join()
    return p.nc


def emit_l1_weight_prepass(p, win, scratch, qw):
    qg, so, qs = p.sem("qg"), p.sem("so"), p.sem("qs")
    with p.scope():
        stg = p.sb("wp_stage", [128, 2 * 1024], BF16)
        for srcw, dst in [(win["w_q"], scratch["qk_bf"]), (win["w_k"], scratch["ov_bf"])]:
            for i in range(8):
                buf = stg[:, (i % 2) * 1024:(i % 2) * 1024 + 1024]
                p.gE.wait_ge(so.sem, max(0, so.v - 16))
                p.gE.dma_start(out=buf, in_=srcw[i * 128:(i + 1) * 128, :]).then_inc(qg.sem, 16)
                qg.inc(16)
                p.yE.wait_ge(qg.sem, qg.v)
                p.yE.dma_start(out=dst[i * 128:(i + 1) * 128, :], in_=buf).then_inc(so.sem, 16)
                so.inc(16)
        f32stg = p.sb("wp_f32", [128, 2816], F32)
        bfstg = p.sb("wp_bf", [128, 2816], BF16)
        for srcw, dst in [(win["w_gate"], scratch["wg_q"]), (win["w_up"], scratch["wu_q"]),
                          (win["w_down"], scratch["wd_q"])]:
            R, C = srcw.shape
            for i in range(R // 128):
                p.yE.wait_ge(p.sem("ve").sem, p.sem("ve").v)
                p.yE.dma_start(out=f32stg[:, 0:C], in_=srcw[i * 128:(i + 1) * 128, :]).then_inc(qs.sem, 16)
                qs.inc(16)
                p.vE.wait_ge(qs.sem, qs.v)
                p.vE.wait_ge(so.sem, so.v)
                if C <= 2048:
                    emit_fp4_quant(p, f32stg[:, 0:C], bfstg[:, 0:C], qw, chan_transpose=False)
                else:
                    emit_fp4_quant(p, f32stg[:, 0:1408], bfstg[:, 0:1408], qw, chan_transpose=False)
                    emit_fp4_quant(p, f32stg[:, 1408:2816], bfstg[:, 1408:2816], qw, chan_transpose=False)
                p.yE.wait_ge(p.sem("ve").sem, p.sem("ve").v)
                p.yE.dma_start(out=dst[i * 128:(i + 1) * 128, :], in_=bfstg[:, 0:C]).then_inc(so.sem, 16)
                so.inc(16)


def build_l1_iso(use_shift=True, dbg=False, projonly=False):
    p = Prog()
    xnb_in = p.dram("xnb", [1024, T], BF16, kind="ExternalInput")
    qk = p.dram("qk", [1024, 1024], F32, kind="ExternalInput")
    ov = p.dram("ov", [1024, 1024], F32, kind="ExternalInput")
    s_out = p.dram("s_out", [1024, T], F32, kind="ExternalOutput")
    scratch = {"qk_bf": p.dram("qk_bf", [1024, 1024], BF16),
               "ov_bf": p.dram("ov_bf", [1024, 1024], BF16)}
    blk = p.es.enter_context(p.nc.Block())
    with p.es:
        @blk.sync
        def _(e):
            p.consts()
            qg, so = p.sem("qg"), p.sem("so")
            with p.scope():
                stg = p.sb("wp_stage", [128, 2 * 1024], BF16)
                for srcw, dst in [(qk, scratch["qk_bf"]), (ov, scratch["ov_bf"])]:
                    for i in range(8):
                        buf = stg[:, (i % 2) * 1024:(i % 2) * 1024 + 1024]
                        p.gE.wait_ge(so.sem, max(0, so.v - 16))
                        p.gE.dma_start(out=buf, in_=srcw[i * 128:(i + 1) * 128, :]).then_inc(qg.sem, 16)
                        qg.inc(16)
                        p.yE.wait_ge(qg.sem, qg.v)
                        p.yE.dma_start(out=dst[i * 128:(i + 1) * 128, :], in_=buf).then_inc(so.sem, 16)
                        so.inc(16)
            p.join()
            with p.scope():
                if dbg:
                    dbg_pe = (p.dram("pe_park", [40 * 128, 512], F32, kind="ExternalOutput"),
                              p.sb("dbg_t", [128, 512], F32)[:],
                              p.dram("y_park", [128, 8 * T], BF16, kind="ExternalOutput"),
                              p.dram("w_park", [128, NT * 1025], BF16, kind="ExternalOutput"),
                              "projonly" if projonly else "all")
                else:
                    dbg_pe = None
                emit_l1_attn(p, xnb_in, scratch, None, None, use_shift=use_shift,
                             dbg_pe=dbg_pe, s_out=s_out)
            p.join()
    return p.nc


def bcastN(col_ap, n):
    return bass.AP(col_ap.tensor, col_ap.offset, [col_ap.ap[0], [0, n]])


def view64(ap2d):
    a0, a1 = ap2d.ap
    assert a1[0] == 1
    return bass.AP(ap2d.tensor, ap2d.offset, [a0, [64, a1[1] // 64], [1, 64]])


def bcast64(col_ap, ngroups):
    return bass.AP(col_ap.tensor, col_ap.offset, list(col_ap.ap) + [[0, 64]])



def emit_l2_attn(p, xnb_dram, scratch, s_dram, parks, dbgd=None):
    qs, qg, mm, ve, ac, so, po = (p.sem(k) for k in ["qs", "qg", "mm", "ve", "ac", "so", "po"])
    w5, wgt_bf = scratch["w5_bf"], scratch["wgt_bf"]
    GN = 16.0
    rk_tm = p.sb("l2_rk", [128, NT], F32)
    rq_tm = p.sb("l2_rq", [128, NT], F32)
    z_fm = p.sb("l2_z", [16, T], F32)
    bc_fm = p.sb("l2_bc", [16, T], F32)
    og_fm = p.sb("l2_og", [128, 8 * T], BF16)
    phiM_row = p.sb("l2_phiM", [1, 3 * 256], F32)
    with p.scope():
        xnb_sb = p.sb("l2a_xnb", [128, NC * 1024], BF16)
        wsb = p.sb("l2a_wsb", [128, NC * 1024], BF16)
        tstage = p.sb("l2a_tst", [128, 1024], BF16)
        sq1 = p.sb("l2a_sq1", [128, 1024], F32)
        rqrow = p.sb("l2a_rqrow", [1, T], F32)
        vsum = p.sb("l2a_vsum", [128, 1], F32)
        ps_a = p.ps("l2a_psa", [128, 512])
        ps_b2 = p.ps("l2a_psb", [128, 512])
        ps_r = p.ps("l2a_psr", [1, 512])
        for th in range(2):
            tb = th * 1024
            p.yE.wait_ge(mm.sem, mm.v)
            for c in range(NC):
                p.yE.dma_start(out=xnb_sb[:, c * 1024:(c + 1) * 1024],
                               in_=xnb_dram[c * 128:(c + 1) * 128, tb:tb + 1024]).then_inc(qs.sem, 16)
                qs.inc(16)
            for kind in ["q", "kfm", "z", "v", "ktm", "g"]:
                p.yE.wait_ge(mm.sem, mm.v)
                if kind == "z":
                    for c in range(NC):
                        p.yE.dma_start(out=wsb[:, c * 16:(c + 1) * 16],
                                       in_=wgt_bf[c * 128:(c + 1) * 128, :]).then_inc(qs.sem, 16)
                        qs.inc(16)
                else:
                    src_row = {"q": 0, "kfm": 1, "v": 2, "ktm": 1, "g": 3}[kind] * 1024
                    for c in range(NC):
                        p.yE.dma_start(out=wsb[:, c * 1024:(c + 1) * 1024],
                                       in_=w5[src_row + c * 128:src_row + (c + 1) * 128, :]).then_inc(qs.sem, 16)
                        qs.inc(16)
                p.tE.wait_ge(qs.sem, qs.v)
                if kind in ("q", "kfm"):
                    dstd = parks["q"] if kind == "q" else parks["kfm"]
                    for et in range(8):
                        for sl in range(2):
                            ps = ps_a if (et * 2 + sl) % 2 == 0 else ps_b2
                            p.tE.wait_ge(ve.sem, ve.v)
                            p.tE.wait_ge(so.sem, so.v)
                            for c in range(NC):
                                mi = p.tE.matmul(
                                    out=ps[:],
                                    lhsT=wsb[:, c * 1024 + et * 128:c * 1024 + et * 128 + 128],
                                    rhs=xnb_sb[:, c * 1024 + sl * 512:c * 1024 + (sl + 1) * 512],
                                    start=(c == 0), stop=(c == NC - 1))
                            mi.then_inc(mm.sem, 1)
                            mm.inc(1)
                            p.vE.wait_ge(mm.sem, mm.v)
                            p.vE.tensor_copy(out=tstage[:, sl * 512:(sl + 1) * 512], in_=ps[:])
                            if kind == "q":
                                p.sE.wait_ge(ve.sem, ve.v)
                                p.sE.activation(out=sq1[:, sl * 512:(sl + 1) * 512],
                                                in_=tstage[:, sl * 512:(sl + 1) * 512], func=AF.Square)
                                p.tE.wait_ge(ac.sem, ac.v)
                                p.tE.matmul(out=ps_r[:], lhsT=p.ones_col_f32[:],
                                            rhs=sq1[:, sl * 512:(sl + 1) * 512],
                                            start=True, stop=True).then_inc(mm.sem, 1)
                                mm.inc(1)
                                p.vE.wait_ge(mm.sem, mm.v)
                                if et == 0:
                                    p.vE.tensor_copy(out=rqrow[:, tb + sl * 512:tb + (sl + 1) * 512],
                                                     in_=ps_r[:])
                                else:
                                    p.vE.tensor_tensor(out=rqrow[:, tb + sl * 512:tb + (sl + 1) * 512],
                                                       in0=rqrow[:, tb + sl * 512:tb + (sl + 1) * 512],
                                                       in1=ps_r[:], op=ALU.add)
                        p.yE.wait_ge(p.sem("ve").sem, p.sem("ve").v)
                        p.yE.dma_start(out=dstd[et * 128:(et + 1) * 128, tb:tb + 1024],
                                       in_=tstage[:]).then_inc(so.sem, 16)
                        so.inc(16)
                elif kind == "z":
                    for sl in range(2):
                        p.tE.wait_ge(ve.sem, ve.v)
                        for c in range(NC):
                            mi = p.tE.matmul(out=ps_a[0:16, :],
                                             lhsT=wsb[:, c * 16:(c + 1) * 16],
                                             rhs=xnb_sb[:, c * 1024 + sl * 512:c * 1024 + (sl + 1) * 512],
                                             start=(c == 0), stop=(c == NC - 1))
                        mi.then_inc(mm.sem, 1)
                        mm.inc(1)
                        p.vE.wait_ge(mm.sem, mm.v)
                        p.vE.tensor_copy(out=z_fm[:, tb + sl * 512:tb + (sl + 1) * 512],
                                         in_=ps_a[0:16, :])
                else:
                    dstd = parks[{"v": "v", "ktm": "ktm", "g": "g"}[kind]]
                    for tt in range(8):
                        gt_ = th * 8 + tt
                        for eg in range(2):
                            ps = ps_a if (tt * 2 + eg) % 2 == 0 else ps_b2
                            p.tE.wait_ge(ve.sem, ve.v)
                            p.tE.wait_ge(so.sem, so.v)
                            for c in range(NC):
                                mi = p.tE.matmul(
                                    out=ps[:],
                                    lhsT=xnb_sb[:, c * 1024 + tt * 128:c * 1024 + tt * 128 + 128],
                                    rhs=wsb[:, c * 1024 + eg * 512:c * 1024 + (eg + 1) * 512],
                                    start=(c == 0), stop=(c == NC - 1))
                            mi.then_inc(mm.sem, 1)
                            mm.inc(1)
                            p.vE.wait_ge(mm.sem, mm.v)
                            p.vE.tensor_copy(out=tstage[:, eg * 512:(eg + 1) * 512], in_=ps[:])
                        if kind == "v":
                            p.sE.wait_ge(ve.sem, ve.v)
                            p.sE.activation(out=sq1[:], in_=tstage[:], func=AF.Square,
                                            accum_out=vsum[:])
                            p.sE.activation(out=vsum[:], in_=vsum[:], func=AF.Ln,
                                            scale=float(1.0 / D), bias=p.cf["eps6"][:])
                            p.sE.activation(out=vsum[:], in_=vsum[:], func=AF.Exp, scale=float(-0.5))
                            p.vE.wait_ge(ac.sem, ac.v)
                            p.vE.tensor_scalar(out=tstage[:], in0=tstage[:], scalar1=vsum[:],
                                               scalar2=None, op0=ALU.mult)
                        if kind == "ktm":
                            p.sE.wait_ge(ve.sem, ve.v)
                            p.sE.activation(out=sq1[:], in_=tstage[:], func=AF.Square,
                                            accum_out=rk_tm[:, gt_:gt_ + 1])
                        p.yE.wait_ge(p.sem("ve").sem, p.sem("ve").v)
                        p.yE.wait_ge(p.sem("ac").sem, p.sem("ac").v)
                        p.yE.dma_start(out=dstd[gt_ * 128:(gt_ + 1) * 128, :],
                                       in_=tstage[:]).then_inc(so.sem, 16)
                        so.inc(16)
        p.sE.wait_ge(ve.sem, ve.v)
        p.sE.activation(out=rqrow[:], in_=rqrow[:], func=AF.Ln, scale=float(1.0 / D),
                        bias=p.cf["eps6"][0:1, :])
        p.sE.activation(out=rqrow[:], in_=rqrow[:], func=AF.Exp, scale=float(-0.5))
        p.sE.activation(out=rk_tm[:], in_=rk_tm[:], func=AF.Ln, scale=float(1.0 / D),
                        bias=p.cf["eps6"][:])
        p.sE.activation(out=rk_tm[:], in_=rk_tm[:], func=AF.Exp, scale=float(-0.5))
        for n_ in range(NT):
            p.tE.wait_ge(ac.sem, ac.v)
            p.tE.wait_ge(ve.sem, ve.v)
            p.tE.transpose(out=ps_a[:, 0:1], in_=rqrow[:, n_ * 128:(n_ + 1) * 128],
                           identity=p.id_f32[0:1, 0:1]).then_inc(mm.sem, 1)
            mm.inc(1)
            p.vE.wait_ge(mm.sem, mm.v)
            p.vE.tensor_copy(out=rq_tm[:, n_:n_ + 1], in_=ps_a[:, 0:1])
        zt = p.sb("l2a_zt", [16, T], F32)[:]
        p.vE.tensor_scalar(out=zt, in0=z_fm[:], scalar1=float(0.0), scalar2=None, op0=ALU.min)
        p.sE.wait_ge(ve.sem, ve.v)
        p.sE.activation(out=z_fm[:], in_=z_fm[:], func=AF.Abs)
        p.sE.activation(out=z_fm[:], in_=z_fm[:], func=AF.Exp, scale=float(-1.0))
        p.sE.activation(out=z_fm[:], in_=z_fm[:], func=AF.Ln, bias=p.cf["one"][0:16, :])
        p.vE.wait_ge(ac.sem, ac.v)
        p.vE.tensor_tensor(out=z_fm[:], in0=zt, in1=z_fm[:], op=ALU.subtract)
        p.vE.tensor_scalar(out=z_fm[:], in0=z_fm[:], scalar1=float(1.0 / GN), scalar2=None,
                           op0=ALU.mult)
        ones16 = p.sb("l2a_ones16", [16, 128], F32)[:]
        p.vE.memset(ones16, 1.0)
        for n_ in range(NT):
            p.vE.tensor_tensor_scan(out=bc_fm[:, n_ * 128:(n_ + 1) * 128],
                                    data0=ones16, data1=z_fm[:, n_ * 128:(n_ + 1) * 128],
                                    initial=0.0, op0=ALU.mult, op1=ALU.add)
        Bc = sq1[0:16, 0:16]
        Mc = sq1[0:16, 16:32]
        PHIc = sq1[0:16, 32:48]
        p.vE.tensor_copy(out=Bc, in_=bass.AP(bc_fm[:].tensor, bc_fm[:].offset + 127,
                                             [bc_fm[:].ap[0], [128, 16]]))
        p.vE.tensor_scalar(out=Mc, in0=Bc, scalar1=float(0.5), scalar2=None, op0=ALU.mult)
        p.vE.tensor_tensor(out=PHIc[:, 0:15], in0=Bc[:, 0:15], in1=Mc[:, 1:16], op=ALU.add)
        p.vE.tensor_copy(out=PHIc[:, 15:16], in_=Bc[:, 15:16])
        p.vE.tensor_tensor(out=PHIc, in0=PHIc, in1=Mc, op=ALU.subtract)
        p.sE.wait_ge(ve.sem, ve.v)
        p.sE.activation(out=PHIc, in_=PHIc, func=AF.Exp)
        bounce = p.dram("l2_bounce", [16, 48], F32)
        p.yE.wait_ge(ac.sem, ac.v)
        p.yE.dma_start(out=bounce, in_=sq1[0:16, 0:48]).then_inc(so.sem, 16)
        so.inc(16)
        p.yE.wait_ge(so.sem, so.v)
        phi_src = bass.AP(bounce.tensor, bounce.offset + 32, [[1, 1], [1, 16], [48, 16]])
        m_src = bass.AP(bounce.tensor, bounce.offset + 16, [[1, 1], [1, 16], [48, 16]])
        p.yE.dma_start(out=phiM_row[:, 0:256].rearrange("o (n h) -> o n h", h=16),
                       in_=phi_src, allow_slow_non_contiguous=True).then_inc(qs.sem, 16)
        qs.inc(16)
        p.yE.dma_start(out=phiM_row[:, 256:512].rearrange("o (n h) -> o n h", h=16),
                       in_=m_src, allow_slow_non_contiguous=True).then_inc(qs.sem, 16)
        qs.inc(16)
        p.vE.wait_ge(qs.sem, qs.v)
        pr = phiM_row[:]
        p.vE.tensor_copy(
            out=bass.AP(pr.tensor, pr.offset + 512, [list(pr.ap[0]), [16, 16], [8, 2], [1, 8]]),
            in_=bass.AP(pr.tensor, pr.offset, [list(pr.ap[0]), [16, 16], [1, 2], [2, 8]]))
        if dbgd is not None:
            p.yE.wait_ge(ve.sem, ve.v)
            p.yE.wait_ge(ac.sem, ac.v)
            for nm, src in [("z_park", z_fm[:]), ("bc_park", bc_fm[:]),
                            ("phiM_park", phiM_row[:]), ("rk_park", rk_tm[:]),
                            ("rq_park", rq_tm[:])]:
                p.yE.dma_start(out=dbgd[nm], in_=src).then_inc(so.sem, 16)
                so.inc(16)
            p.yE.wait_ge(so.sem, so.v)
    p.join()
    with p.scope():
        kfm_sb = p.sb("l2c_kfm", [128, 2 * 1024], BF16)
        ktm_sb = p.sb("l2c_ktm", [128, 2 * 1024], BF16)
        v_sb = p.sb("l2c_v", [128, 2 * 1024], BF16)
        q_sb = p.sb("l2c_q", [128, 2 * 1024], BF16)
        g_sb = p.sb("l2c_g", [128, 2 * 1024], BF16)
        bcm = p.sb("l2c_bcm", [128, 16], F32)
        a_tm = p.sb("l2c_a", [128, 16], F32)
        binvk = p.sb("l2c_bk", [128, 16], F32)
        aq = p.sb("l2c_aq", [128, 16], F32)
        kb = p.sb("l2c_kb", [128, 1024], BF16)
        AT = p.sb("l2c_AT", [128, 2 * 128], BF16)
        o32 = p.sb("l2c_o32", [128, 1024], F32)
        sqg = p.sb("l2c_sqg", [128, 1024], F32)
        ss16 = p.sb("l2c_ss", [128, 16], F32)
        ogst = p.sb("l2c_ogst", [128, 1024], BF16)
        St = p.sb("l2c_St", [128, 512], F32)
        Stb = p.sb("l2c_Stb", [128, 512], BF16)
        phib = p.sb("l2c_phib", [128, 8], F32)
        ps_sT1 = p.ps("l2c_psT1", [128, 128])
        ps_sT2 = p.ps("l2c_psT2", [128, 128])
        ps_o = p.ps("l2c_pso", [128, 1024])
        ps_kv = p.ps("l2c_pskv", [128, 512])
        ps_b = p.ps("l2c_psbc", [128, 16])
        ps_t = p.ps("l2c_pst", [128, 128])
        ps_tb = p.ps("l2c_pstb", [128, 128], BF16)
        p.vE.memset(St[:], 0.0)
        p.vE.memset(Stb[:], 0.0)
        at_ve_bank = [0, 0]
        for n_ in range(NT):
            par = n_ % 2
            p.yE.wait_ge(mm.sem, mm.v)
            for src, dst in [(parks["kfm"], kfm_sb), (parks["q"], q_sb)]:
                for r in range(8):
                    p.yE.dma_start(out=dst[:, par * 1024 + r * 128:par * 1024 + (r + 1) * 128],
                                   in_=src[r * 128:(r + 1) * 128, n_ * 128:(n_ + 1) * 128]).then_inc(qs.sem, 16)
                    qs.inc(16)
            for src, dst in [(parks["ktm"], ktm_sb), (parks["v"], v_sb), (parks["g"], g_sb)]:
                p.yE.dma_start(out=dst[:, par * 1024:(par + 1) * 1024],
                               in_=src[n_ * 128:(n_ + 1) * 128, :]).then_inc(qs.sem, 16)
                qs.inc(16)
            p.tE.wait_ge(ve.sem, ve.v)
            p.tE.transpose(out=ps_t[:, 0:16], in_=bc_fm[:, n_ * 128:(n_ + 1) * 128],
                           identity=p.id_f32[0:16, 0:16]).then_inc(mm.sem, 1)
            mm.inc(1)
            p.tE.matmul(out=ps_b[:], lhsT=p.ones_p1_f32[:],
                        rhs=phiM_row[:, 256 + n_ * 16:256 + (n_ + 1) * 16],
                        start=True, stop=True).then_inc(mm.sem, 1)
            mm.inc(1)
            p.vE.wait_ge(mm.sem, mm.v)
            p.vE.tensor_copy(out=bcm[:], in_=ps_t[:, 0:16])
            p.vE.tensor_tensor(out=bcm[:], in0=bcm[:], in1=ps_b[:], op=ALU.subtract)
            p.sE.wait_ge(ve.sem, ve.v)
            p.sE.activation(out=a_tm[:], in_=bcm[:], func=AF.Exp)
            p.sE.activation(out=binvk[:], in_=bcm[:], func=AF.Exp, scale=float(-1.0))
            p.vE.wait_ge(ac.sem, ac.v)
            p.vE.scalar_tensor_tensor(out=aq[:], in0=a_tm[:], scalar=float(1.0 / 8.0),
                                      in1=bcastN(rq_tm[:, n_:n_ + 1], 16),
                                      op0=ALU.mult, op1=ALU.mult)
            p.vE.tensor_tensor(out=binvk[:], in0=binvk[:],
                               in1=bcastN(rk_tm[:, n_:n_ + 1], 16), op=ALU.mult)
            p.vE.wait_ge(qs.sem, qs.v)
            p.vE.tensor_tensor(out=view64(kb[:]),
                               in0=view64(ktm_sb[:, par * 1024:(par + 1) * 1024]),
                               in1=bcast64(binvk[:], 16), op=ALU.mult)
            for h in range(16):
                hp, ho = h // 2, (h % 2) * 64
                psT = ps_sT1 if h % 2 == 0 else ps_sT2
                p.tE.wait_ge(ve.sem, ve.v)
                p.tE.matmul(out=psT[:],
                            lhsT=kfm_sb[ho:ho + 64, par * 1024 + hp * 128:par * 1024 + hp * 128 + 128],
                            rhs=q_sb[ho:ho + 64, par * 1024 + hp * 128:par * 1024 + hp * 128 + 128],
                            start=True, stop=True).then_inc(mm.sem, 1)
                mm.inc(1)
                p.vE.wait_ge(mm.sem, mm.v)
                p.vE.scalar_tensor_tensor(out=AT[:, (h % 2) * 128:(h % 2) * 128 + 128],
                                          in0=psT[:], scalar=binvk[:, h:h + 1],
                                          in1=p.triu_i[:], op0=ALU.mult, op1=ALU.mult)
                at_ve_bank[h % 2] = ve.v
                p.tE.wait_ge(ve.sem, ve.v)
                p.tE.matmul(out=ps_o[:, h * 64:(h + 1) * 64],
                            lhsT=AT[:, (h % 2) * 128:(h % 2) * 128 + 128],
                            rhs=v_sb[:, par * 1024 + h * 64:par * 1024 + (h + 1) * 64],
                            start=True, stop=False)
                p.tE.matmul(out=ps_o[:, h * 64:(h + 1) * 64],
                            lhsT=q_sb[ho:ho + 64, par * 1024 + hp * 128:par * 1024 + hp * 128 + 128],
                            rhs=Stb[ho:ho + 64, hp * 64:(hp + 1) * 64],
                            start=False, stop=True)
                p.tE.matmul(out=ps_kv[ho:ho + 64, hp * 64:(hp + 1) * 64],
                            lhsT=kb[:, h * 64:(h + 1) * 64],
                            rhs=v_sb[:, par * 1024 + h * 64:par * 1024 + (h + 1) * 64],
                            start=True, stop=True).then_inc(mm.sem, 1)
                mm.inc(1)
            p.vE.wait_ge(mm.sem, mm.v)
            p.vE.tensor_tensor(out=view64(o32[:]),
                               in0=bass.AP(ps_o[:].tensor, ps_o[:].offset,
                                           [ps_o[:].ap[0], [64, 16], [1, 64]]),
                               in1=bcast64(aq[:], 16), op=ALU.mult)
            if dbgd is not None:
                p.yE.wait_ge(ve.sem, ve.v)
                p.yE.dma_start(out=dbgd["o32_park"][n_ * 128:(n_ + 1) * 128, :],
                               in_=o32[:]).then_inc(so.sem, 16)
                so.inc(16)
                p.sE.wait_ge(so.sem, so.v)
            p.sE.wait_ge(ve.sem, ve.v)
            p.sE.activation(out=sqg[:], in_=o32[:], func=AF.Square)
            p.vE.wait_ge(ac.sem, ac.v)
            p.vE.tensor_reduce(out=ss16[:], in_=view64(sqg[:]), axis=AX.X, op=ALU.add)
            p.sE.wait_ge(ve.sem, ve.v)
            p.sE.activation(out=ss16[:], in_=ss16[:], func=AF.Ln, scale=float(1.0 / 64.0),
                            bias=p.cf["eps6"][:])
            p.sE.activation(out=ss16[:], in_=ss16[:], func=AF.Exp, scale=float(-0.5))
            p.sE.activation(out=sqg[:], in_=g_sb[:, par * 1024:(par + 1) * 1024], func=AF.Silu)
            p.vE.wait_ge(ac.sem, ac.v)
            p.vE.tensor_tensor(out=view64(o32[:]), in0=view64(o32[:]),
                               in1=bcast64(ss16[:], 16), op=ALU.mult)
            p.vE.tensor_tensor(out=ogst[:], in0=o32[:], in1=sqg[:], op=ALU.mult)
            for cchunk in range(8):
                p.tE.wait_ge(ve.sem, ve.v)
                p.tE.transpose(out=ps_tb[:], in_=ogst[:, cchunk * 128:(cchunk + 1) * 128],
                               identity=p.id_bf[:]).then_inc(mm.sem, 1)
                mm.inc(1)
                p.vE.wait_ge(mm.sem, mm.v)
                p.vE.tensor_copy(out=og_fm[:, cchunk * T + n_ * 128:cchunk * T + (n_ + 1) * 128],
                                 in_=ps_tb[:])
            p.tE.wait_ge(ve.sem, ve.v)
            p.tE.matmul(out=ps_b[:, 0:8], lhsT=p.ones_p1_f32[:],
                        rhs=phiM_row[:, 512 + n_ * 16:512 + n_ * 16 + 8],
                        start=True, stop=True)
            p.tE.matmul(out=ps_b[:, 8:16], lhsT=p.ones_p1_f32[:],
                        rhs=phiM_row[:, 512 + n_ * 16 + 8:512 + n_ * 16 + 16],
                        start=True, stop=True).then_inc(mm.sem, 1)
            mm.inc(1)
            p.vE.wait_ge(mm.sem, mm.v)
            p.vE.tensor_copy(out=phib[:], in_=ps_b[:, 0:8])
            p.vE.tensor_copy(out=phib[64:128, :], in_=ps_b[64:128, 8:16])
            p.vE.tensor_tensor(out=St[:], in0=St[:], in1=ps_kv[:], op=ALU.add)
            p.vE.tensor_tensor(out=view64(St[:]), in0=view64(St[:]),
                               in1=bcast64(phib[:], 8), op=ALU.mult)
            p.vE.tensor_copy(out=Stb[:], in_=St[:])
            if dbgd is not None:
                p.yE.wait_ge(ve.sem, ve.v)
                p.yE.dma_start(out=dbgd["st_park"][n_ * 128:(n_ + 1) * 128, :],
                               in_=St[:]).then_inc(so.sem, 16)
                so.inc(16)
                p.vE.wait_ge(so.sem, so.v)
        if dbgd is not None:
            p.yE.wait_ge(ve.sem, ve.v)
            for c in range(8):
                p.yE.dma_start(out=dbgd["og_park"][c * 128:(c + 1) * 128, :],
                               in_=og_fm[:, c * T:(c + 1) * T]).then_inc(so.sem, 16)
                so.inc(16)
            p.yE.wait_ge(so.sem, so.v)
    p.join()
    with p.scope():
        wsb = p.sb("l2e_wsb", [128, NC * 1024], BF16)
        sstg = p.sb("l2e_sstg", [128, 512], BF16)
        ps_s = p.ps("l2e_pss", [128, 512])
        ps_s2 = p.ps("l2e_pss2", [128, 512])
        for c in range(NC):
            p.yE.dma_start(out=wsb[:, c * 1024:(c + 1) * 1024],
                           in_=w5[4 * 1024 + c * 128:4 * 1024 + (c + 1) * 128, :]).then_inc(qs.sem, 16)
            qs.inc(16)
        p.tE.wait_ge(qs.sem, qs.v)
        for dc in range(NC):
            for sl in range(4):
                ps = ps_s if (dc * 4 + sl) % 2 == 0 else ps_s2
                p.tE.wait_ge(ve.sem, ve.v)
                p.tE.wait_ge(so.sem, so.v)
                for ec in range(NC):
                    mi = p.tE.matmul(out=ps[:],
                                     lhsT=wsb[:, ec * 1024 + dc * 128:ec * 1024 + dc * 128 + 128],
                                     rhs=og_fm[:, ec * T + sl * 512:ec * T + (sl + 1) * 512],
                                     start=(ec == 0), stop=(ec == NC - 1))
                mi.then_inc(mm.sem, 1)
                mm.inc(1)
                p.vE.wait_ge(mm.sem, mm.v)
                p.vE.tensor_copy(out=sstg[:], in_=ps[:])
                p.yE.wait_ge(p.sem("ve").sem, p.sem("ve").v)
                p.yE.dma_start(out=s_dram[dc * 128:(dc + 1) * 128, sl * 512:(sl + 1) * 512],
                               in_=sstg[:]).then_inc(so.sem, 16)
                so.inc(16)



def emit_router(p, lg_ps, rb_sb, mask_fm, gate_row, gated):
    mm, ve, ac = p.sem("mm"), p.sem("ve"), p.sem("ac")
    with p.scope():
        lg = p.sb("rt_lg", [16, T], F32)
        lgt = p.sb("rt_lgt", [128, NT * 8], F32)
        mx = p.sb("rt_mx", [128, NT], F32)
        msk = p.sb("rt_msk", [128, NT * 8], F32)
        gt = p.sb("rt_gt", [128, NT], F32)
        nmx = p.sb("rt_nmx", [128, 1], F32)
        ps_t = p.ps("rt_pst", [128, 128])
        p.vE.wait_ge(mm.sem, mm.v)
        for sl in range(4):
            p.vE.tensor_copy(out=lg[0:8, sl * 512:(sl + 1) * 512],
                             in_=lg_ps[:, sl * 512:(sl + 1) * 512])
            p.vE.tensor_scalar(out=lg[0:8, sl * 512:(sl + 1) * 512],
                               in0=lg[0:8, sl * 512:(sl + 1) * 512],
                               scalar1=rb_sb[:], scalar2=None, op0=ALU.add)
        for i in range(NT):
            p.tE.wait_ge(ve.sem, ve.v)
            p.tE.transpose(out=ps_t[:, 0:8], in_=lg[0:8, i * 128:(i + 1) * 128],
                           identity=p.id_f32[0:8, 0:8]).then_inc(mm.sem, 1)
            mm.inc(1)
            p.vE.wait_ge(mm.sem, mm.v)
            p.vE.tensor_copy(out=lgt[:, i * 8:(i + 1) * 8], in_=ps_t[:, 0:8])
            p.vE.reduce_max(out=mx[:, i:i + 1], in_=lgt[:, i * 8:(i + 1) * 8], axis=AX.X)
            p.vE.tensor_scalar(out=msk[:, i * 8:(i + 1) * 8], in0=lgt[:, i * 8:(i + 1) * 8],
                               scalar1=mx[:, i:i + 1], scalar2=float(0.0),
                               op0=ALU.subtract, op1=ALU.is_ge)
            if gated:
                p.vE.tensor_scalar(out=msk[:, i * 8 + 8 - 8:i * 8 + 8] if False else nmx[:, 0:1],
                                   in0=mx[:, i:i + 1], scalar1=float(-1.0), scalar2=None,
                                   op0=ALU.mult)
                p.sE.wait_ge(ve.sem, ve.v)
                p.sE.activation(out=lgt[:, i * 8:(i + 1) * 8], in_=lgt[:, i * 8:(i + 1) * 8],
                                func=AF.Exp, bias=nmx[:, 0:1],
                                accum_out=gt[:, i:i + 1])
        if gated:
            p.sE.activation(out=gt[:], in_=gt[:], func=AF.Ln)
            p.sE.activation(out=gt[:], in_=gt[:], func=AF.Exp, scale=float(-1.0))
            p.vE.wait_ge(ac.sem, ac.v)
            for i in range(NT):
                p.vE.tensor_scalar(out=msk[:, i * 8:(i + 1) * 8], in0=msk[:, i * 8:(i + 1) * 8],
                                   scalar1=gt[:, i:i + 1], scalar2=None, op0=ALU.mult)
        for i in range(NT):
            p.tE.wait_ge(ve.sem, ve.v)
            p.tE.transpose(out=ps_t[0:8, :], in_=msk[:, i * 8:(i + 1) * 8],
                           identity=p.id_f32[:]).then_inc(mm.sem, 1)
            mm.inc(1)
            p.vE.wait_ge(mm.sem, mm.v)
            p.vE.tensor_copy(out=mask_fm[0:8, i * 128:(i + 1) * 128], in_=ps_t[0:8, :])



def emit_moe_precast(p, src, dst, rows, cols, stage):
    pc = p.pcast
    cc = min(cols, 1024)
    for e in range(8):
        for i in range(rows // 128):
            for j in range(cols // cc):
                buf = stage[:, (pc.nbuf % 2) * 1024:(pc.nbuf % 2) * 1024 + cc]
                if pc.nbuf >= 2:
                    p.gE.wait_ge(pc.sem, pc.store_t[pc.nbuf - 2])
                p.gE.dma_start(
                    out=buf,
                    in_=src[e, i * 128:(i + 1) * 128, j * cc:(j + 1) * cc]).then_inc(pc.sem, 16)
                pc.inc(16)
                p.gE.wait_ge(pc.sem, pc.v)
                p.gE.dma_start(
                    out=dst[e, i * 128:(i + 1) * 128, j * cc:(j + 1) * cc],
                    in_=buf).then_inc(pc.sem, 16)
                pc.inc(16)
                pc.store_t[pc.nbuf] = pc.v
                pc.nbuf += 1


def emit_moe(p, xnb_dram, w1_d, b1_d, w2_d, b2_d, mask_fm, s_dram, x_fm, act,
             pcast_done=None, dbgd=None):
    qs, qg, mm, ve, ac, so = (p.sem(k) for k in ["qs", "qg", "mm", "ve", "ac", "so"])
    if pcast_done:
        p.gE.wait_ge(p.pcast.sem, pcast_done)
        p.yE.wait_ge(p.pcast.sem, pcast_done)
    with p.scope():
        xnb_sb = p.sb("moe_xnb", [128, NC * 512], BF16)
        h_sb = p.sb("moe_h", [128, 32 * 512], BF16)
        acc = p.sb("moe_acc", [128, 8 * 512], F32)
        w_sb = p.sb("moe_w", [128, 4 * 4096], BF16)
        b1all = p.sb("moe_b1", [128, 8 * 32], F32)
        b2all = p.sb("moe_b2", [128, 8 * 8], F32)
        maskb = p.sb("moe_maskb", [128, 512], BF16)
        mrow = p.sb("moe_mrow", [1, 512], BF16)
        cmbt = p.sb("moe_cmb", [128, 512], F32)
        sstg = p.sb("moe_sstg", [128, 512], BF16)
        ps_h1 = p.ps("moe_psh1", [128, 512])
        ps_h2 = p.ps("moe_psh2", [128, 512])
        ps_y = p.ps("moe_psy", [128, 4 * 512])
        ps_m = p.ps("moe_psm", [128, 512])
        bstg1 = p.sb("moe_bstg1", [32, 8 * 128], F32)
        bstg2 = p.sb("moe_bstg2", [8, 8 * 128], F32)
        for e in range(8):
            p.yE.dma_start(out=bstg1[:, e * 128:(e + 1) * 128],
                           in_=bass.AP(b1_d.tensor, b1_d.offset + e * 4096,
                                       [[128, 32], [1, 128]])).then_inc(qs.sem, 16)
            qs.inc(16)
            p.yE.dma_start(out=bstg2[:, e * 128:(e + 1) * 128],
                           in_=bass.AP(b2_d.tensor, b2_d.offset + e * 1024,
                                       [[128, 8], [1, 128]])).then_inc(qs.sem, 16)
            qs.inc(16)
        p.tE.wait_ge(qs.sem, qs.v)
        for e in range(8):
            p.tE.wait_ge(ve.sem, ve.v)
            p.tE.transpose(out=ps_m[:, 0:32], in_=bstg1[:, e * 128:(e + 1) * 128],
                           identity=p.id_f32[0:32, 0:32])
            ti = p.tE.transpose(out=ps_m[:, 32:40], in_=bstg2[:, e * 128:(e + 1) * 128],
                                identity=p.id_f32[0:8, 0:8])
            ti.then_inc(mm.sem, 1)
            mm.inc(1)
            p.vE.wait_ge(mm.sem, mm.v)
            p.vE.tensor_copy(out=b1all[:, e * 32:(e + 1) * 32], in_=ps_m[:, 0:32])
            p.vE.tensor_copy(out=b2all[:, e * 8:(e + 1) * 8], in_=ps_m[:, 32:40])
        gidx = 0
        mm_after = []
        ld_after = []
        mm_w1_last = 0
        ac_bank = [0, 0]
        ve_accfree = 0
        ve_maskfree = ve.v
        ac_hchunk = {}
        for ts_ in range(4):
            t0 = ts_ * 512
            p.yE.wait_ge(mm.sem, mm_w1_last)
            for c in range(NC):
                p.yE.dma_start(out=xnb_sb[:, c * 512:(c + 1) * 512],
                               in_=xnb_dram[c * 128:(c + 1) * 128, t0:t0 + 512]).then_inc(qs.sem, 16)
                qs.inc(16)
            for e in range(8):
                p.yE.dma_start(out=mrow[:], in_=mask_fm[e:e + 1, t0:t0 + 512]).then_inc(qs.sem, 16)
                qs.inc(16)
                p.tE.wait_ge(qs.sem, qs.v)
                p.tE.wait_ge(ve.sem, ve_maskfree)
                p.tE.matmul(out=ps_m[:], lhsT=p.ones_p1_bf[:],
                            rhs=mrow[:], start=True, stop=True).then_inc(mm.sem, 1)
                mm.inc(1)
                p.vE.wait_ge(mm.sem, mm.v)
                p.vE.tensor_copy(out=maskb[:], in_=ps_m[:])
                ve_maskfree = ve.v
                for fg in range(8):
                    half = w_sb[:, (gidx % 4) * 4096:(gidx % 4) * 4096 + 4096]
                    if gidx >= 4:
                        p.gE.wait_ge(mm.sem, mm_after[gidx - 4])
                    for c in range(NC):
                        p.gE.dma_start(
                            out=half[:, c * 512:(c + 1) * 512],
                            in_=w1_d[e, c * 128:(c + 1) * 128, fg * 512:(fg + 1) * 512]).then_inc(qg.sem, 16)
                        qg.inc(16)
                    ld_after.append((qg, qg.v))
                    lctr, ltgt = ld_after[gidx]
                    p.tE.wait_ge(lctr.sem, ltgt)
                    for ft in range(4):
                        fgl = fg * 4 + ft
                        ps_h = ps_h1 if fgl % 2 == 0 else ps_h2
                        p.tE.wait_ge(ac.sem, ac_bank[fgl % 2])
                        for c in range(NC):
                            mi = p.tE.matmul(out=ps_h[:],
                                             lhsT=half[:, c * 512 + ft * 128:c * 512 + ft * 128 + 128],
                                             rhs=xnb_sb[:, c * 512:(c + 1) * 512],
                                             start=(c == 0), stop=(c == NC - 1))
                        mi.then_inc(mm.sem, 1)
                        mm.inc(1)
                        p.sE.wait_ge(mm.sem, mm.v)
                        p.sE.activation(out=h_sb[:, fgl * 512:(fgl + 1) * 512], in_=ps_h[:],
                                        func=act,
                                        bias=b1all[:, e * 32 + fgl:e * 32 + fgl + 1])
                        ac_bank[fgl % 2] = ac.v
                        ac_hchunk[fgl] = ac.v
                    mm_after.append(mm.v)
                    gidx += 1
                if e == 7:
                    mm_w1_last = mm.v
                for dh in range(2):
                    for fh in range(4):
                        half = w_sb[:, (gidx % 4) * 4096:(gidx % 4) * 4096 + 4096]
                        if gidx >= 4:
                            p.yE.wait_ge(mm.sem, mm_after[gidx - 4])
                        for fc in range(8):
                            p.yE.dma_start(
                                out=half[:, fc * 512:(fc + 1) * 512],
                                in_=w2_d[e, fh * 1024 + fc * 128:fh * 1024 + (fc + 1) * 128,
                                         dh * 512:(dh + 1) * 512]).then_inc(qs.sem, 16)
                            qs.inc(16)
                        ld_after.append((qs, qs.v))
                        lctr, ltgt = ld_after[gidx]
                        p.tE.wait_ge(lctr.sem, ltgt)
                        p.tE.wait_ge(ac.sem, ac_hchunk[fh * 8 + 7])
                        if fh == 0:
                            p.tE.wait_ge(ve.sem, ve_accfree)
                        for dt in range(4):
                            for fc in range(8):
                                mi = p.tE.matmul(
                                    out=ps_y[:, dt * 512:(dt + 1) * 512],
                                    lhsT=half[:, fc * 512 + dt * 128:fc * 512 + dt * 128 + 128],
                                    rhs=h_sb[:, (fh * 8 + fc) * 512:(fh * 8 + fc + 1) * 512],
                                    start=(fh == 0 and fc == 0), stop=(fh == 3 and fc == 7))
                        mi.then_inc(mm.sem, 1)
                        mm.inc(1)
                        mm_after.append(mm.v)
                        gidx += 1
                    p.vE.wait_ge(mm.sem, mm.v)
                    for dt in range(4):
                        dgl = dh * 4 + dt
                        if e == 0:
                            p.vE.scalar_tensor_tensor(
                                out=acc[:, dgl * 512:(dgl + 1) * 512],
                                in0=ps_y[:, dt * 512:(dt + 1) * 512],
                                scalar=b2all[:, e * 8 + dgl:e * 8 + dgl + 1],
                                in1=maskb[:], op0=ALU.add, op1=ALU.mult)
                        else:
                            p.vE.scalar_tensor_tensor(
                                out=cmbt[:],
                                in0=ps_y[:, dt * 512:(dt + 1) * 512],
                                scalar=b2all[:, e * 8 + dgl:e * 8 + dgl + 1],
                                in1=maskb[:], op0=ALU.add, op1=ALU.mult)
                            p.vE.tensor_tensor(out=acc[:, dgl * 512:(dgl + 1) * 512],
                                               in0=acc[:, dgl * 512:(dgl + 1) * 512],
                                               in1=cmbt[:], op=ALU.add)
                    ve_accfree = ve.v
            for dc in range(8):
                p.sE.wait_ge(ve.sem, ve.v)
                p.sE.activation(out=cmbt[:], in_=acc[:, dc * 512:(dc + 1) * 512], func=AF.Silu)
                p.gE.wait_ge(ve.sem, ve.v)
                p.gE.dma_start(out=sstg[:], in_=s_dram[dc * 128:(dc + 1) * 128,
                                                       t0:t0 + 512]).then_inc(qg.sem, 16)
                qg.inc(16)
                p.vE.wait_ge(ac.sem, ac.v)
                p.vE.wait_ge(qg.sem, qg.v)
                p.vE.tensor_tensor(out=cmbt[:], in0=cmbt[:], in1=sstg[:], op=ALU.mult)
                xsl = x_fm[:, dc * T + t0:dc * T + t0 + 512]
                p.vE.tensor_tensor(out=xsl, in0=xsl, in1=cmbt[:], op=ALU.add)


def build_layer2(dbg=False):
    p = Prog()
    x_in = p.dram("x", [T, D], F32, kind="ExternalInput")
    Wq = p.dram("Wq", [1024, 1024], F32, kind="ExternalInput")
    Wk = p.dram("Wk", [1024, 1024], F32, kind="ExternalInput")
    Wv = p.dram("Wv", [1024, 1024], F32, kind="ExternalInput")
    Wg = p.dram("Wg", [1024, 1024], F32, kind="ExternalInput")
    Wgt = p.dram("Wgt", [1024, 16], F32, kind="ExternalInput")
    Wo = p.dram("Wo", [1024, 1024], F32, kind="ExternalInput")
    r_w = p.dram("router_w", [1024, 8], F32, kind="ExternalInput")
    r_b = p.dram("router_b", [8], F32, kind="ExternalInput")
    W1 = p.dram("W1", [8, 1024, 4096], F32, kind="ExternalInput")
    b1 = p.dram("b1", [8, 4096], F32, kind="ExternalInput")
    W2 = p.dram("W2", [8, 4096, 1024], F32, kind="ExternalInput")
    b2 = p.dram("b2", [8, 1024], F32, kind="ExternalInput")
    x_out = p.dram("x_out", [T, D], F32, kind="ExternalOutput")
    scratch = {
        "w5_bf": p.dram("w5_bf", [5 * 1024, 1024], BF16),
        "wgt_bf": p.dram("wgt_bf", [1024, 16], BF16),
    }
    kind_dbg = "ExternalOutput" if dbg else "Internal"
    xnb_dram = p.dram("xnb_park", [1024, T], BF16, kind=kind_dbg)
    s_dram = p.dram("s_park", [1024, T], BF16, kind=kind_dbg)
    parks = {nm: p.dram(f"park_{nm}", [1024, T] if nm in ("q", "kfm") else [T, 1024], BF16,
                        kind=kind_dbg)
             for nm in ["q", "kfm", "ktm", "v", "g"]}
    dbgd = None
    if dbg:
        dbgd = {"z_park": p.dram("z_park", [16, T], F32, kind="ExternalOutput"),
                "bc_park": p.dram("bc_park", [16, T], F32, kind="ExternalOutput"),
                "phiM_park": p.dram("phiM_park", [1, 768], F32, kind="ExternalOutput"),
                "rk_park": p.dram("rk_park", [128, NT], F32, kind="ExternalOutput"),
                "rq_park": p.dram("rq_park", [128, NT], F32, kind="ExternalOutput"),
                "mask_park": p.dram("mask_park", [8, T], BF16, kind="ExternalOutput"),
                "og_park": p.dram("og_park", [1024, T], BF16, kind="ExternalOutput"),
                "gmoe_park": p.dram("gmoe_park", [1024, T], F32, kind="ExternalOutput"),
                "o32_park": p.dram("o32_park", [T, 1024], F32, kind="ExternalOutput"),
                "st_park": p.dram("st_park", [NT * 128, 512], F32, kind="ExternalOutput"),
                "h_park": p.dram("h_park", [128, 32 * 512], BF16, kind="ExternalOutput"),
                "acc0_park": p.dram("acc0_park", [128, 8 * 512], F32, kind="ExternalOutput"),
                "y0_park": p.dram("y0_park", [128, 2 * 2048], F32, kind="ExternalOutput")}
    x_fm = p.sb("x_fm", [128, NC * T], F32)
    mask_fm = p.sb("l2_mask", [8, T], BF16)
    blk = p.es.enter_context(p.nc.Block())
    with p.es:
        @blk.sync
        def _(e):
            p.consts()
            qs, qg, so = p.sem("qs"), p.sem("qg"), p.sem("so")
            with p.scope():
                stg = p.sb("wp_stage", [128, 2 * 1024], BF16)
                for wi, srcw in enumerate([Wq, Wk, Wv, Wg, Wo]):
                    for i in range(8):
                        buf = stg[:, (i % 2) * 1024:(i % 2) * 1024 + 1024]
                        p.gE.wait_ge(so.sem, max(0, so.v - 16))
                        p.gE.dma_start(out=buf, in_=srcw[i * 128:(i + 1) * 128, :]).then_inc(qg.sem, 16)
                        qg.inc(16)
                        p.yE.wait_ge(qg.sem, qg.v)
                        p.yE.dma_start(out=scratch["w5_bf"][wi * 1024 + i * 128:wi * 1024 + (i + 1) * 128, :],
                                       in_=buf).then_inc(so.sem, 16)
                        so.inc(16)
                for i in range(8):
                    buf = stg[:, (i % 2) * 1024:(i % 2) * 1024 + 16]
                    p.gE.wait_ge(so.sem, max(0, so.v - 16))
                    p.gE.dma_start(out=buf, in_=Wgt[i * 128:(i + 1) * 128, :]).then_inc(qg.sem, 16)
                    qg.inc(16)
                    p.yE.wait_ge(qg.sem, qg.v)
                    p.yE.dma_start(out=scratch["wgt_bf"][i * 128:(i + 1) * 128, :],
                                   in_=buf).then_inc(so.sem, 16)
                    so.inc(16)
            p.join()
            with p.scope():
                stage = p.sb("stage", [128, 2 * D], F32)
                pb = p.ps("pb", [128, 1024])
                p.tE.wait_ge(p.sem("po").sem, p.sem("po").v)
                emit_load_x_fm(p, x_in, x_fm[:], stage[:], pb[:])
            p.join()
            with p.scope():
                lw = make_ln_work(p)
                rw_sb = p.sb("l2_rw", [128, NC * 8], F32)
                rb_sb = p.sb("l2_rb", [8, 1], F32)
                lg_ps = p.ps("l2_lgps", [8, T])
                xnb_c = p.sb("l2_xnbc", [128, T], BF16)
                for c in range(NC):
                    p.yE.dma_start(out=rw_sb[:, c * 8:(c + 1) * 8],
                                   in_=r_w[c * 128:(c + 1) * 128, :]).then_inc(qs.sem, 16)
                    qs.inc(16)
                p.yE.dma_start(out=rb_sb[:], in_=r_b.rearrange("(a b) -> a b", b=1)).then_inc(qs.sem, 16)
                qs.inc(16)
                p.tE.wait_ge(qs.sem, qs.v)
                p.vE.wait_ge(qs.sem, qs.v)
                emit_ln_parked(p, x_fm[:], xnb_c[:], None, lw, xnb_dram, None,
                               router=(rw_sb[:], lg_ps[:]))
                emit_router(p, lg_ps[:], rb_sb[:], mask_fm[:], None, gated=False)
                if dbg:
                    p.yE.wait_ge(p.sem("ve").sem, p.sem("ve").v)
                    p.yE.dma_start(out=dbgd["mask_park"], in_=mask_fm[:]).then_inc(qs.sem, 16)
                    qs.inc(16)
                    p.yE.wait_ge(qs.sem, qs.v)
            p.join()
            with p.scope():
                emit_l2_attn(p, xnb_dram, scratch, s_dram, parks, dbgd=dbgd)
            p.join()
            emit_moe(p, xnb_dram, W1, b1, W2, b2, mask_fm[:], s_dram, x_fm[:], AF.Gelu,
                     dbgd=dbgd)
            p.join()
            with p.scope():
                stage = p.sb("stage2", [128, 2 * D], F32)
                pb = p.ps("pb2", [128, 1024])
                emit_store_fm(p, x_fm[:], x_out, stage[:], pb[:])
            p.join()
    return p.nc


def emit_l3_attn(p, xnb_dram, scratch, s_dram, parks, dbgd=None):
    qs, qg, mm, ve, ac, so, po = (p.sem(k) for k in ["qs", "qg", "mm", "ve", "ac", "so", "po"])
    w4, wbt, bb_sb = scratch["w4_bf"], scratch["wbt_bf"], scratch["bb_sb"]
    bt_fm = p.sb("l3_bt", [16, T], F32)
    og_fm = p.sb("l3_og", [128, 8 * T], BF16)
    with p.scope():
        xnb_sb = p.sb("l3a_xnb", [128, NC * 1024], BF16)
        wsb = p.sb("l3a_wsb", [128, NC * 1024], BF16)
        tstage = p.sb("l3a_tst", [128, 1024], BF16)
        sq1 = p.sb("l3a_sq1", [128, 1024], F32)
        rn16 = p.sb("l3a_rn16", [128, 16], F32)
        ps_a = p.ps("l3a_psa", [128, 512])
        ps_b2 = p.ps("l3a_psb", [128, 512])
        for th in range(2):
            tb = th * 1024
            p.yE.wait_ge(mm.sem, mm.v)
            for c in range(NC):
                p.yE.dma_start(out=xnb_sb[:, c * 1024:(c + 1) * 1024],
                               in_=xnb_dram[c * 128:(c + 1) * 128, tb:tb + 1024]).then_inc(qs.sem, 16)
                qs.inc(16)
            for kind in ["q", "z", "v", "ktm"]:
                p.yE.wait_ge(mm.sem, mm.v)
                if kind == "z":
                    for c in range(NC):
                        p.yE.dma_start(out=wsb[:, c * 16:(c + 1) * 16],
                                       in_=wbt[c * 128:(c + 1) * 128, :]).then_inc(qs.sem, 16)
                        qs.inc(16)
                else:
                    src_row = {"q": 0, "ktm": 1, "v": 2}[kind] * 1024
                    for c in range(NC):
                        p.yE.dma_start(out=wsb[:, c * 1024:(c + 1) * 1024],
                                       in_=w4[src_row + c * 128:src_row + (c + 1) * 128, :]).then_inc(qs.sem, 16)
                        qs.inc(16)
                p.tE.wait_ge(qs.sem, qs.v)
                if kind == "q":
                    dstd = parks["q"]
                    for et in range(8):
                        for sl in range(2):
                            ps = ps_a if (et * 2 + sl) % 2 == 0 else ps_b2
                            p.tE.wait_ge(ve.sem, ve.v)
                            p.tE.wait_ge(so.sem, so.v)
                            for c in range(NC):
                                mi = p.tE.matmul(
                                    out=ps[:],
                                    lhsT=wsb[:, c * 1024 + et * 128:c * 1024 + et * 128 + 128],
                                    rhs=xnb_sb[:, c * 1024 + sl * 512:c * 1024 + (sl + 1) * 512],
                                    start=(c == 0), stop=(c == NC - 1))
                            mi.then_inc(mm.sem, 1)
                            mm.inc(1)
                            p.vE.wait_ge(mm.sem, mm.v)
                            p.vE.tensor_copy(out=tstage[:, sl * 512:(sl + 1) * 512], in_=ps[:])
                        p.yE.wait_ge(ve.sem, ve.v)
                        p.yE.dma_start(out=dstd[et * 128:(et + 1) * 128, tb:tb + 1024],
                                       in_=tstage[:]).then_inc(so.sem, 16)
                        so.inc(16)
                elif kind == "z":
                    for sl in range(2):
                        p.tE.wait_ge(ve.sem, ve.v)
                        for c in range(NC):
                            mi = p.tE.matmul(out=ps_a[0:16, :],
                                             lhsT=wsb[:, c * 16:(c + 1) * 16],
                                             rhs=xnb_sb[:, c * 1024 + sl * 512:c * 1024 + (sl + 1) * 512],
                                             start=(c == 0), stop=(c == NC - 1))
                        mi.then_inc(mm.sem, 1)
                        mm.inc(1)
                        p.vE.wait_ge(mm.sem, mm.v)
                        p.vE.tensor_copy(out=bt_fm[:, tb + sl * 512:tb + (sl + 1) * 512],
                                         in_=ps_a[0:16, :])
                else:
                    dstd = parks[kind]
                    for tt in range(8):
                        gt_ = th * 8 + tt
                        for eg in range(2):
                            ps = ps_a if (tt * 2 + eg) % 2 == 0 else ps_b2
                            p.tE.wait_ge(ve.sem, ve.v)
                            p.tE.wait_ge(so.sem, so.v)
                            for c in range(NC):
                                mi = p.tE.matmul(
                                    out=ps[:],
                                    lhsT=xnb_sb[:, c * 1024 + tt * 128:c * 1024 + tt * 128 + 128],
                                    rhs=wsb[:, c * 1024 + eg * 512:c * 1024 + (eg + 1) * 512],
                                    start=(c == 0), stop=(c == NC - 1))
                            mi.then_inc(mm.sem, 1)
                            mm.inc(1)
                            p.vE.wait_ge(mm.sem, mm.v)
                            p.vE.tensor_copy(out=tstage[:, eg * 512:(eg + 1) * 512], in_=ps[:])
                        if kind == "ktm":
                            p.sE.wait_ge(ve.sem, ve.v)
                            p.sE.activation(out=sq1[:], in_=tstage[:], func=AF.Square)
                            p.vE.wait_ge(ac.sem, ac.v)
                            p.vE.tensor_reduce(out=rn16[:], in_=view64(sq1[:]), axis=AX.X,
                                               op=ALU.add)
                            p.sE.wait_ge(ve.sem, ve.v)
                            p.sE.activation(out=rn16[:], in_=rn16[:], func=AF.Ln,
                                            bias=p.cf["eps24"][:])
                            p.sE.activation(out=rn16[:], in_=rn16[:], func=AF.Exp,
                                            scale=float(-0.5))
                            p.vE.wait_ge(ac.sem, ac.v)
                            p.vE.tensor_tensor(out=view64(tstage[:]), in0=view64(tstage[:]),
                                               in1=bcast64(rn16[:], 16), op=ALU.mult)
                        p.yE.wait_ge(ve.sem, ve.v)
                        p.yE.wait_ge(ac.sem, ac.v)
                        p.yE.dma_start(out=dstd[gt_ * 128:(gt_ + 1) * 128, :],
                                       in_=tstage[:]).then_inc(so.sem, 16)
                        so.inc(16)
        for sl in range(4):
            p.vE.tensor_scalar(out=bt_fm[:, sl * 512:(sl + 1) * 512],
                               in0=bt_fm[:, sl * 512:(sl + 1) * 512],
                               scalar1=bb_sb[:], scalar2=None, op0=ALU.add)
        p.sE.wait_ge(ve.sem, ve.v)
        p.sE.activation(out=bt_fm[:], in_=bt_fm[:], func=AF.Sigmoid)
        if dbgd is not None:
            p.yE.wait_ge(ac.sem, ac.v)
            p.yE.dma_start(out=dbgd["bt_park"], in_=bt_fm[:]).then_inc(so.sem, 16)
            so.inc(16)
            p.yE.wait_ge(so.sem, so.v)
    p.join()
    with p.scope():
        ktm_sb = p.sb("l3c_ktm", [128, 2 * 1024], BF16)
        v_sb = p.sb("l3c_v", [128, 2 * 1024], BF16)
        q_sb = p.sb("l3c_q", [128, 2 * 1024], BF16)
        kfm_sb = p.sb("l3c_kfm", [128, 1024], BF16)
        bt_tm = p.sb("l3c_btm", [128, 16], F32)
        nbt = p.sb("l3c_nbt", [128, 16], F32)
        P_sb = p.sb("l3c_P", [128, 16 * 128], BF16)
        PT_sb = p.sb("l3c_PT", [128, 16 * 128], BF16)
        X32 = p.sb("l3c_X32", [128, 16 * 64], F32)
        Xb = p.sb("l3c_Xb", [128, 16 * 64], BF16)
        AT = p.sb("l3c_AT", [128, 128], BF16)
        ogst = p.sb("l3c_ogst", [128, 1024], BF16)
        St = p.sb("l3c_St", [128, 512], F32)
        Stb = p.sb("l3c_Stb", [128, 512], BF16)
        ps_o = p.ps("l3c_pso", [128, 1024])
        ps_kv = p.ps("l3c_pskv", [128, 512])
        psq1 = p.ps("l3c_psq1", [128, 512])
        psq2 = p.ps("l3c_psq2", [128, 512])
        ps_tb = p.ps("l3c_pstb", [128, 128], BF16)
        ps_tb2 = p.ps("l3c_pstb2", [128, 128], BF16)
        ps_sm = p.ps("l3c_pssm", [128, 512])
        p.vE.memset(St[:], 0.0)
        p.vE.memset(Stb[:], 0.0)
        mm_hist = {}
        for n_ in range(NT):
            par = n_ % 2
            p.yE.wait_ge(mm.sem, mm_hist.get(n_ - 2, 0))
            for r in range(8):
                p.yE.dma_start(out=q_sb[:, par * 1024 + r * 128:par * 1024 + (r + 1) * 128],
                               in_=parks["q"][r * 128:(r + 1) * 128,
                                              n_ * 128:(n_ + 1) * 128]).then_inc(qs.sem, 16)
                qs.inc(16)
            for src, dst in [(parks["ktm"], ktm_sb), (parks["v"], v_sb)]:
                p.yE.dma_start(out=dst[:, par * 1024:(par + 1) * 1024],
                               in_=src[n_ * 128:(n_ + 1) * 128, :]).then_inc(qs.sem, 16)
                qs.inc(16)
            p.tE.wait_ge(ve.sem, ve.v)
            p.tE.transpose(out=ps_sm[:, 0:16], in_=bt_fm[:, n_ * 128:(n_ + 1) * 128],
                           identity=p.id_f32[0:16, 0:16]).then_inc(mm.sem, 1)
            mm.inc(1)
            p.vE.wait_ge(mm.sem, mm.v)
            p.vE.tensor_copy(out=bt_tm[:], in_=ps_sm[:, 0:16])
            p.vE.tensor_scalar(out=nbt[:], in0=bt_tm[:], scalar1=float(-1.0), scalar2=None,
                               op0=ALU.mult)
            p.tE.wait_ge(qs.sem, qs.v)
            pre_ve = ve.v
            kt_v = {}
            for r in range(8):
                tbr = ps_tb if r % 2 == 0 else ps_tb2
                p.tE.wait_ge(ve.sem, kt_v[r - 2] if r >= 2 else pre_ve)
                p.tE.transpose(out=tbr[:],
                               in_=ktm_sb[:, par * 1024 + r * 128:par * 1024 + (r + 1) * 128],
                               identity=p.id_bf[:]).then_inc(mm.sem, 1)
                mm.inc(1)
                p.vE.wait_ge(mm.sem, mm.v)
                p.vE.tensor_copy(out=kfm_sb[:, r * 128:(r + 1) * 128], in_=tbr[:])
                kt_v[r] = ve.v
            for h in range(16):
                hp, ho = h // 2, (h % 2) * 64
                kfm_h = kfm_sb[ho:ho + 64, hp * 128:(hp + 1) * 128]
                q_h = q_sb[ho:ho + 64, par * 1024 + hp * 128:par * 1024 + hp * 128 + 128]
                ktm_h = ktm_sb[:, par * 1024 + h * 64:par * 1024 + (h + 1) * 64]
                v_h = v_sb[:, par * 1024 + h * 64:par * 1024 + (h + 1) * 64]
                St_h = Stb[ho:ho + 64, hp * 64:(hp + 1) * 64]
                Ph = P_sb[:, h * 128:(h + 1) * 128]
                PTh = PT_sb[:, h * 128:(h + 1) * 128]
                Xb_h = Xb[:, h * 64:(h + 1) * 64]
                X32_h = X32[:, h * 64:(h + 1) * 64]
                p.tE.wait_ge(ve.sem, ve.v)
                p.tE.matmul(out=psq1[:, 0:128], lhsT=kfm_h, rhs=kfm_h,
                            start=True, stop=True)
                p.tE.matmul(out=ps_sm[:, 64:128], lhsT=kfm_h, rhs=St_h,
                            start=True, stop=True)
                p.tE.matmul(out=psq2[:, 128:256], lhsT=kfm_h, rhs=q_h,
                            start=True, stop=True).then_inc(mm.sem, 1)
                mm.inc(1)
                p.vE.wait_ge(mm.sem, mm.v)
                p.vE.scalar_tensor_tensor(out=Ph, in0=psq1[:, 0:128], scalar=nbt[:, h:h + 1],
                                          in1=p.tril_s[:], op0=ALU.mult, op1=ALU.mult)
                p.vE.tensor_tensor(out=X32_h, in0=v_h, in1=ps_sm[:, 64:128], op=ALU.subtract)
                p.vE.tensor_scalar(out=X32_h, in0=X32_h, scalar1=bt_tm[:, h:h + 1],
                                   scalar2=None, op0=ALU.mult)
                p.vE.tensor_copy(out=Xb_h, in_=X32_h)
                p.vE.tensor_tensor(out=AT[:], in0=psq2[:, 128:256], in1=p.triu_i[:], op=ALU.mult)
                p.tE.wait_ge(ve.sem, ve.v)
                p.tE.transpose(out=ps_tb[:], in_=Ph, identity=p.id_bf[:]).then_inc(mm.sem, 1)
                mm.inc(1)
                p.vE.wait_ge(mm.sem, mm.v)
                p.vE.tensor_copy(out=PTh, in_=ps_tb[:])
                for lvl in range(7):
                    p.tE.wait_ge(ve.sem, ve.v)
                    mi = p.tE.matmul(out=ps_sm[:, 128:192], lhsT=PTh, rhs=Xb_h,
                                     start=True, stop=True)
                    if lvl < 6:
                        mi = p.tE.matmul(out=psq1[:, 0:128], lhsT=Ph, rhs=PTh,
                                         start=True, stop=True)
                        if lvl < 5:
                            mi = p.tE.matmul(out=psq2[:, 0:128], lhsT=PTh, rhs=Ph,
                                             start=True, stop=True)
                    mi.then_inc(mm.sem, 1)
                    mm.inc(1)
                    p.vE.wait_ge(mm.sem, mm.v)
                    p.vE.tensor_tensor(out=X32_h, in0=X32_h, in1=ps_sm[:, 128:192], op=ALU.add)
                    p.vE.tensor_copy(out=Xb_h, in_=X32_h)
                    if lvl < 6:
                        p.vE.tensor_copy(out=PTh, in_=psq1[:, 0:128])
                        if lvl < 5:
                            p.vE.tensor_copy(out=Ph, in_=psq2[:, 0:128])
                p.tE.wait_ge(ve.sem, ve.v)
                p.tE.matmul(out=ps_o[:, h * 64:(h + 1) * 64], lhsT=q_h, rhs=St_h,
                            start=True, stop=False)
                p.tE.matmul(out=ps_o[:, h * 64:(h + 1) * 64], lhsT=AT[:], rhs=Xb_h,
                            start=False, stop=True)
                p.tE.matmul(out=ps_kv[ho:ho + 64, hp * 64:(hp + 1) * 64], lhsT=ktm_h,
                            rhs=Xb_h, start=True, stop=True).then_inc(mm.sem, 1)
                mm.inc(1)
            p.vE.wait_ge(mm.sem, mm.v)
            p.vE.tensor_copy(out=ogst[:], in_=ps_o[:])
            pre_ve = ve.v
            og_v = {}
            for cchunk in range(8):
                tbr = ps_tb if cchunk % 2 == 0 else ps_tb2
                p.tE.wait_ge(ve.sem, og_v[cchunk - 2] if cchunk >= 2 else pre_ve)
                p.tE.transpose(out=tbr[:], in_=ogst[:, cchunk * 128:(cchunk + 1) * 128],
                               identity=p.id_bf[:]).then_inc(mm.sem, 1)
                mm.inc(1)
                p.vE.wait_ge(mm.sem, mm.v)
                p.vE.tensor_copy(out=og_fm[:, cchunk * T + n_ * 128:cchunk * T + (n_ + 1) * 128],
                                 in_=tbr[:])
                og_v[cchunk] = ve.v
            p.vE.tensor_tensor(out=St[:], in0=St[:], in1=ps_kv[:], op=ALU.add)
            p.vE.tensor_copy(out=Stb[:], in_=St[:])
            if dbgd is not None:
                p.yE.wait_ge(ve.sem, ve.v)
                p.yE.dma_start(out=dbgd["st3_park"][n_ * 128:(n_ + 1) * 128, :],
                               in_=St[:]).then_inc(so.sem, 16)
                so.inc(16)
                p.vE.wait_ge(so.sem, so.v)
            mm_hist[n_] = mm.v
        if dbgd is not None:
            p.yE.wait_ge(ve.sem, ve.v)
            for c in range(8):
                p.yE.dma_start(out=dbgd["og3_park"][c * 128:(c + 1) * 128, :],
                               in_=og_fm[:, c * T:(c + 1) * T]).then_inc(so.sem, 16)
                so.inc(16)
            p.yE.wait_ge(so.sem, so.v)
    p.join()
    with p.scope():
        wsb = p.sb("l3e_wsb", [128, NC * 1024], BF16)
        sstg = p.sb("l3e_sstg", [128, 512], BF16)
        ps_s = p.ps("l3e_pss", [128, 512])
        ps_s2 = p.ps("l3e_pss2", [128, 512])
        for c in range(NC):
            p.yE.dma_start(out=wsb[:, c * 1024:(c + 1) * 1024],
                           in_=w4[3 * 1024 + c * 128:3 * 1024 + (c + 1) * 128, :]).then_inc(qs.sem, 16)
            qs.inc(16)
        p.tE.wait_ge(qs.sem, qs.v)
        for dc in range(NC):
            for sl in range(4):
                ps = ps_s if (dc * 4 + sl) % 2 == 0 else ps_s2
                p.tE.wait_ge(ve.sem, ve.v)
                p.tE.wait_ge(so.sem, so.v)
                for ec in range(NC):
                    mi = p.tE.matmul(out=ps[:],
                                     lhsT=wsb[:, ec * 1024 + dc * 128:ec * 1024 + dc * 128 + 128],
                                     rhs=og_fm[:, ec * T + sl * 512:ec * T + (sl + 1) * 512],
                                     start=(ec == 0), stop=(ec == NC - 1))
                mi.then_inc(mm.sem, 1)
                mm.inc(1)
                p.vE.wait_ge(mm.sem, mm.v)
                p.vE.tensor_copy(out=sstg[:], in_=ps[:])
                p.yE.wait_ge(p.sem("ve").sem, p.sem("ve").v)
                p.yE.dma_start(out=s_dram[dc * 128:(dc + 1) * 128, sl * 512:(sl + 1) * 512],
                               in_=sstg[:]).then_inc(so.sem, 16)
                so.inc(16)


def build_layer3(dbg=False):
    p = Prog()
    x_in = p.dram("x", [T, D], F32, kind="ExternalInput")
    Wq = p.dram("Wq", [1024, 1024], F32, kind="ExternalInput")
    Wk = p.dram("Wk", [1024, 1024], F32, kind="ExternalInput")
    Wv = p.dram("Wv", [1024, 1024], F32, kind="ExternalInput")
    Wbeta = p.dram("Wbeta", [1024, 16], F32, kind="ExternalInput")
    b_beta = p.dram("b_beta", [16], F32, kind="ExternalInput")
    Wout = p.dram("Wout", [1024, 1024], F32, kind="ExternalInput")
    r_w = p.dram("router_w", [1024, 8], F32, kind="ExternalInput")
    r_b = p.dram("router_b", [8], F32, kind="ExternalInput")
    W1 = p.dram("W1", [8, 1024, 4096], F32, kind="ExternalInput")
    b1 = p.dram("b1", [8, 4096], F32, kind="ExternalInput")
    W2 = p.dram("W2", [8, 4096, 1024], F32, kind="ExternalInput")
    b2 = p.dram("b2", [8, 1024], F32, kind="ExternalInput")
    x_out = p.dram("x_out", [T, D], F32, kind="ExternalOutput")
    scratch = {
        "w4_bf": p.dram("w4_bf", [4 * 1024, 1024], BF16),
        "wbt_bf": p.dram("wbt_bf", [1024, 16], BF16),
    }
    kind_dbg = "ExternalOutput" if dbg else "Internal"
    xnb_dram = p.dram("xnb_park", [1024, T], BF16, kind=kind_dbg)
    s_dram = p.dram("s_park", [1024, T], BF16, kind=kind_dbg)
    parks = {nm: p.dram(f"park_{nm}", [1024, T] if nm == "q" else [T, 1024], BF16,
                        kind=kind_dbg)
             for nm in ["q", "ktm", "v"]}
    dbgd = None
    if dbg:
        dbgd = {"bt_park": p.dram("bt_park", [16, T], F32, kind="ExternalOutput"),
                "st3_park": p.dram("st3_park", [NT * 128, 512], F32, kind="ExternalOutput"),
                "og3_park": p.dram("og3_park", [1024, T], BF16, kind="ExternalOutput"),
                "mask_park": p.dram("mask_park", [8, T], BF16, kind="ExternalOutput"),
                "gmoe_park": p.dram("gmoe_park", [1024, T], F32, kind="ExternalOutput")}
    x_fm = p.sb("x_fm", [128, NC * T], F32)
    mask_fm = p.sb("l3_mask", [8, T], BF16)
    bb_sb = p.sb("l3_bb", [16, 1], F32)
    scratch["bb_sb"] = bb_sb[:]
    blk = p.es.enter_context(p.nc.Block())
    with p.es:
        @blk.sync
        def _(e):
            p.consts()
            qs, qg, so = p.sem("qs"), p.sem("qg"), p.sem("so")
            with p.scope():
                stg = p.sb("wp_stage", [128, 2 * 1024], BF16)
                for wi, srcw in enumerate([Wq, Wk, Wv, Wout]):
                    for i in range(8):
                        buf = stg[:, (i % 2) * 1024:(i % 2) * 1024 + 1024]
                        p.gE.wait_ge(so.sem, max(0, so.v - 16))
                        p.gE.dma_start(out=buf, in_=srcw[i * 128:(i + 1) * 128, :]).then_inc(qg.sem, 16)
                        qg.inc(16)
                        p.yE.wait_ge(qg.sem, qg.v)
                        p.yE.dma_start(out=scratch["w4_bf"][wi * 1024 + i * 128:wi * 1024 + (i + 1) * 128, :],
                                       in_=buf).then_inc(so.sem, 16)
                        so.inc(16)
                for i in range(8):
                    buf = stg[:, (i % 2) * 1024:(i % 2) * 1024 + 16]
                    p.gE.wait_ge(so.sem, max(0, so.v - 16))
                    p.gE.dma_start(out=buf, in_=Wbeta[i * 128:(i + 1) * 128, :]).then_inc(qg.sem, 16)
                    qg.inc(16)
                    p.yE.wait_ge(qg.sem, qg.v)
                    p.yE.dma_start(out=scratch["wbt_bf"][i * 128:(i + 1) * 128, :],
                                   in_=buf).then_inc(so.sem, 16)
                    so.inc(16)
                p.yE.dma_start(out=bb_sb[:], in_=b_beta.rearrange("(a b) -> a b", b=1)).then_inc(qs.sem, 16)
                qs.inc(16)
                p.vE.wait_ge(qs.sem, qs.v)
            p.join()
            with p.scope():
                stage = p.sb("stage", [128, 2 * D], F32)
                pb = p.ps("pb", [128, 1024])
                p.tE.wait_ge(p.sem("po").sem, p.sem("po").v)
                emit_load_x_fm(p, x_in, x_fm[:], stage[:], pb[:])
            p.join()
            with p.scope():
                lw = make_ln_work(p)
                rw_sb = p.sb("l3_rw", [128, NC * 8], F32)
                rb_sb = p.sb("l3_rb", [8, 1], F32)
                lg_ps = p.ps("l3_lgps", [8, T])
                xnb_c = p.sb("l3_xnbc", [128, T], BF16)
                for c in range(NC):
                    p.yE.dma_start(out=rw_sb[:, c * 8:(c + 1) * 8],
                                   in_=r_w[c * 128:(c + 1) * 128, :]).then_inc(qs.sem, 16)
                    qs.inc(16)
                p.yE.dma_start(out=rb_sb[:], in_=r_b.rearrange("(a b) -> a b", b=1)).then_inc(qs.sem, 16)
                qs.inc(16)
                p.tE.wait_ge(qs.sem, qs.v)
                p.vE.wait_ge(qs.sem, qs.v)
                emit_ln_parked(p, x_fm[:], xnb_c[:], None, lw, xnb_dram, None,
                               router=(rw_sb[:], lg_ps[:]))
                emit_router(p, lg_ps[:], rb_sb[:], mask_fm[:], None, gated=True)
                if dbg:
                    p.yE.wait_ge(p.sem("ve").sem, p.sem("ve").v)
                    p.yE.dma_start(out=dbgd["mask_park"], in_=mask_fm[:]).then_inc(qs.sem, 16)
                    qs.inc(16)
                    p.yE.wait_ge(qs.sem, qs.v)
            p.join()
            with p.scope():
                emit_l3_attn(p, xnb_dram, scratch, s_dram, parks, dbgd=dbgd)
            p.join()
            emit_moe(p, xnb_dram, W1, b1, W2, b2, mask_fm[:], s_dram, x_fm[:], AF.Relu,
                     dbgd=dbgd)
            p.join()
            with p.scope():
                stage = p.sb("stage2", [128, 2 * D], F32)
                pb = p.ps("pb2", [128, 1024])
                emit_store_fm(p, x_fm[:], x_out, stage[:], pb[:])
            p.join()
    return p.nc



def emit_layer0_body(p, x_fm, vin, precast=None):
    scratch = {
        "wqkv_bf": p.dram("wqkv_bf", [3072, 1024], BF16),
        "wo_bf": p.dram("wo_bf", [1024, 1024], BF16),
        "wg_q": p.dram("wg_q", [2816, 1024], BF16),
        "wu_q": p.dram("wu_q", [2816, 1024], BF16),
        "wd_q": p.dram("wd_q", [1024, 2816], BF16),
    }
    xnb_dram = p.dram("xnb_park", [1024, T], BF16)
    xq_dram = p.dram("xq_park", [1024, T], BF16)
    g_dram = p.dram("g_park", [1024, T], BF16)
    with p.scope():
        qw = make_quant_work(p)
        emit_l01_weight_prepass(p, vin, scratch, qw)
        if precast is not None:
            precast()
        with p.scope():
            lw = make_ln_work(p)
            lw.update(qw)
            xnb_c = p.sb("l0_xnbc", [128, T], BF16)
            xq_c = p.sb("l0_xqc", [128, T], BF16)
            emit_ln_parked(p, x_fm, xnb_c[:], xq_c[:], lw, xnb_dram, xq_dram)
        p.join()
        emit_ffn_fp4(p, xq_dram, g_dram, scratch, qw)
        p.join()
    with p.scope():
        emit_l0_attn(p, xnb_dram, scratch, g_dram, x_fm)
    p.join()


def emit_layer1_body(p, x_fm, vin, precast=None):
    scratch = {
        "qk_bf": p.dram("qk_bf", [1024, 1024], BF16),
        "ov_bf": p.dram("ov_bf", [1024, 1024], BF16),
        "wg_q": p.dram("wg_q", [2816, 1024], BF16),
        "wu_q": p.dram("wu_q", [2816, 1024], BF16),
        "wd_q": p.dram("wd_q", [1024, 2816], BF16),
    }
    xnb_dram = p.dram("xnb_park", [1024, T], BF16)
    xq_dram = p.dram("xq_park", [1024, T], BF16)
    g_dram = p.dram("g_park", [1024, T], BF16)
    with p.scope():
        qw = make_quant_work(p)
        win = {"w_q": vin["qk"], "w_k": vin["ov"], "w_gate": vin["w_gate"],
               "w_up": vin["w_up"], "w_down": vin["w_down"]}
        sc2 = {"wqkv_bf": None, "qk_bf": scratch["qk_bf"], "ov_bf": scratch["ov_bf"],
               "wg_q": scratch["wg_q"], "wu_q": scratch["wu_q"], "wd_q": scratch["wd_q"]}
        emit_l1_weight_prepass(p, win, sc2, qw)
        if precast is not None:
            precast()
        with p.scope():
            lw = make_ln_work(p)
            lw.update(qw)
            xnb_c = p.sb("l1_xnbc", [128, T], BF16)
            xq_c = p.sb("l1_xqc", [128, T], BF16)
            emit_ln_parked(p, x_fm, xnb_c[:], xq_c[:], lw, xnb_dram, xq_dram)
        p.join()
        emit_ffn_fp4(p, xq_dram, g_dram, scratch, qw)
        p.join()
    with p.scope():
        emit_l1_attn(p, xnb_dram, scratch, g_dram, x_fm, use_shift=True)
    p.join()


def emit_layer2_body(p, x_fm, vin, mask_fm, moe_w=None):
    qs, qg, so = p.sem("qs"), p.sem("qg"), p.sem("so")
    scratch = {
        "w5_bf": p.dram("w5_bf", [5 * 1024, 1024], BF16),
        "wgt_bf": p.dram("wgt_bf", [1024, 16], BF16),
    }
    xnb_dram = p.dram("xnb_park", [1024, T], BF16)
    s_dram = p.dram("s_park", [1024, T], BF16)
    parks = {nm: p.dram(f"park_{nm}", [1024, T] if nm in ("q", "kfm") else [T, 1024], BF16)
             for nm in ["q", "kfm", "ktm", "v", "g"]}
    with p.scope():
        stg = p.sb("wp_stage", [128, 2 * 1024], BF16)
        for wi, srcw in enumerate([vin["Wq"], vin["Wk"], vin["Wv"], vin["Wg"], vin["Wo"]]):
            for i in range(8):
                buf = stg[:, (i % 2) * 1024:(i % 2) * 1024 + 1024]
                p.gE.wait_ge(so.sem, max(0, so.v - 16))
                p.gE.dma_start(out=buf, in_=srcw[i * 128:(i + 1) * 128, :]).then_inc(qg.sem, 16)
                qg.inc(16)
                p.yE.wait_ge(qg.sem, qg.v)
                p.yE.dma_start(out=scratch["w5_bf"][wi * 1024 + i * 128:wi * 1024 + (i + 1) * 128, :],
                               in_=buf).then_inc(so.sem, 16)
                so.inc(16)
        for i in range(8):
            buf = stg[:, (i % 2) * 1024:(i % 2) * 1024 + 16]
            p.gE.wait_ge(so.sem, max(0, so.v - 16))
            p.gE.dma_start(out=buf, in_=vin["Wgt"][i * 128:(i + 1) * 128, :]).then_inc(qg.sem, 16)
            qg.inc(16)
            p.yE.wait_ge(qg.sem, qg.v)
            p.yE.dma_start(out=scratch["wgt_bf"][i * 128:(i + 1) * 128, :],
                           in_=buf).then_inc(so.sem, 16)
            so.inc(16)
    p.join()
    with p.scope():
        lw = make_ln_work(p)
        rw_sb = p.sb("l2_rw", [128, NC * 8], F32)
        rb_sb = p.sb("l2_rb", [8, 1], F32)
        lg_ps = p.ps("l2_lgps", [8, T])
        xnb_c = p.sb("l2_xnbc", [128, T], BF16)
        for c in range(NC):
            p.yE.dma_start(out=rw_sb[:, c * 8:(c + 1) * 8],
                           in_=vin["router_w"][c * 128:(c + 1) * 128, :]).then_inc(qs.sem, 16)
            qs.inc(16)
        p.yE.dma_start(out=rb_sb[:], in_=vin["router_b"].rearrange("(a b) -> a b", b=1)).then_inc(qs.sem, 16)
        qs.inc(16)
        p.tE.wait_ge(qs.sem, qs.v)
        p.vE.wait_ge(qs.sem, qs.v)
        emit_ln_parked(p, x_fm, xnb_c[:], None, lw, xnb_dram, None,
                       router=(rw_sb[:], lg_ps[:]))
        emit_router(p, lg_ps[:], rb_sb[:], mask_fm, None, gated=False)
    p.join()
    with p.scope():
        emit_l2_attn(p, xnb_dram, scratch, s_dram, parks)
    p.join()
    w1s, w2s, pcd = moe_w if moe_w is not None else (vin["W1"], vin["W2"], None)
    emit_moe(p, xnb_dram, w1s, vin["b1"], w2s, vin["b2"], mask_fm,
             s_dram, x_fm, AF.Gelu, pcast_done=pcd)
    p.join()


def emit_layer3_body(p, x_fm, vin, mask_fm, bb_sb, moe_w=None):
    qs, qg, so = p.sem("qs"), p.sem("qg"), p.sem("so")
    scratch = {
        "w4_bf": p.dram("w4_bf", [4 * 1024, 1024], BF16),
        "wbt_bf": p.dram("wbt_bf", [1024, 16], BF16),
        "bb_sb": bb_sb,
    }
    xnb_dram = p.dram("xnb_park", [1024, T], BF16)
    s_dram = p.dram("s_park", [1024, T], BF16)
    parks = {nm: p.dram(f"park_{nm}", [1024, T] if nm == "q" else [T, 1024], BF16)
             for nm in ["q", "ktm", "v"]}
    with p.scope():
        stg = p.sb("wp_stage", [128, 2 * 1024], BF16)
        for wi, srcw in enumerate([vin["Wq"], vin["Wk"], vin["Wv"], vin["Wout"]]):
            for i in range(8):
                buf = stg[:, (i % 2) * 1024:(i % 2) * 1024 + 1024]
                p.gE.wait_ge(so.sem, max(0, so.v - 16))
                p.gE.dma_start(out=buf, in_=srcw[i * 128:(i + 1) * 128, :]).then_inc(qg.sem, 16)
                qg.inc(16)
                p.yE.wait_ge(qg.sem, qg.v)
                p.yE.dma_start(out=scratch["w4_bf"][wi * 1024 + i * 128:wi * 1024 + (i + 1) * 128, :],
                               in_=buf).then_inc(so.sem, 16)
                so.inc(16)
        for i in range(8):
            buf = stg[:, (i % 2) * 1024:(i % 2) * 1024 + 16]
            p.gE.wait_ge(so.sem, max(0, so.v - 16))
            p.gE.dma_start(out=buf, in_=vin["Wbeta"][i * 128:(i + 1) * 128, :]).then_inc(qg.sem, 16)
            qg.inc(16)
            p.yE.wait_ge(qg.sem, qg.v)
            p.yE.dma_start(out=scratch["wbt_bf"][i * 128:(i + 1) * 128, :],
                           in_=buf).then_inc(so.sem, 16)
            so.inc(16)
        p.yE.dma_start(out=bb_sb, in_=vin["b_beta"].rearrange("(a b) -> a b", b=1)).then_inc(qs.sem, 16)
        qs.inc(16)
        p.vE.wait_ge(qs.sem, qs.v)
    p.join()
    with p.scope():
        lw = make_ln_work(p)
        rw_sb = p.sb("l3_rw", [128, NC * 8], F32)
        rb_sb = p.sb("l3_rb", [8, 1], F32)
        lg_ps = p.ps("l3_lgps", [8, T])
        xnb_c = p.sb("l3_xnbc", [128, T], BF16)
        for c in range(NC):
            p.yE.dma_start(out=rw_sb[:, c * 8:(c + 1) * 8],
                           in_=vin["router_w"][c * 128:(c + 1) * 128, :]).then_inc(qs.sem, 16)
            qs.inc(16)
        p.yE.dma_start(out=rb_sb[:], in_=vin["router_b"].rearrange("(a b) -> a b", b=1)).then_inc(qs.sem, 16)
        qs.inc(16)
        p.tE.wait_ge(qs.sem, qs.v)
        p.vE.wait_ge(qs.sem, qs.v)
        emit_ln_parked(p, x_fm, xnb_c[:], None, lw, xnb_dram, None,
                       router=(rw_sb[:], lg_ps[:]))
        emit_router(p, lg_ps[:], rb_sb[:], mask_fm, None, gated=True)
    p.join()
    with p.scope():
        emit_l3_attn(p, xnb_dram, scratch, s_dram, parks)
    p.join()
    w1s, w2s, pcd = moe_w if moe_w is not None else (vin["W1"], vin["W2"], None)
    emit_moe(p, xnb_dram, w1s, vin["b1"], w2s, vin["b2"], mask_fm,
             s_dram, x_fm, AF.Relu, pcast_done=pcd)
    p.join()


def build_fused():
    p = Prog()
    x_in = p.dram("x", [T, D], F32, kind="ExternalInput")
    EI = "ExternalInput"
    v0 = {k: p.dram("l0_" + k, sh, F32, kind=EI) for k, sh in [
        ("w_q", [1024, 1024]), ("w_k", [1024, 1024]), ("w_v", [1024, 1024]),
        ("w_o", [1024, 1024]), ("w_gate", [2816, 1024]), ("w_up", [2816, 1024]),
        ("w_down", [1024, 2816])]}
    v1 = {k: p.dram("l1_" + k, sh, F32, kind=EI) for k, sh in [
        ("qk", [1024, 1024]), ("ov", [1024, 1024]), ("w_gate", [2816, 1024]),
        ("w_up", [2816, 1024]), ("w_down", [1024, 2816])]}
    v2 = {k: p.dram("l2_" + k, sh, F32, kind=EI) for k, sh in [
        ("Wq", [1024, 1024]), ("Wk", [1024, 1024]), ("Wv", [1024, 1024]),
        ("Wg", [1024, 1024]), ("Wgt", [1024, 16]), ("Wo", [1024, 1024]),
        ("router_w", [1024, 8]), ("router_b", [8]),
        ("W1", [8, 1024, 4096]), ("b1", [8, 4096]),
        ("W2", [8, 4096, 1024]), ("b2", [8, 1024])]}
    v3 = {k: p.dram("l3_" + k, sh, F32, kind=EI) for k, sh in [
        ("Wq", [1024, 1024]), ("Wk", [1024, 1024]), ("Wv", [1024, 1024]),
        ("Wbeta", [1024, 16]), ("b_beta", [16]), ("Wout", [1024, 1024]),
        ("router_w", [1024, 8]), ("router_b", [8]),
        ("W1", [8, 1024, 4096]), ("b1", [8, 4096]),
        ("W2", [8, 4096, 1024]), ("b2", [8, 1024])]}
    x_out = p.dram("x_out", [T, D], F32, kind="ExternalOutput")
    x_fm = p.sb("x_fm", [128, NC * T], F32)
    mask_fm = p.sb("mask_fm", [8, T], BF16)
    bb_sb = p.sb("l3_bb", [16, 1], F32)
    w1q2 = p.dram("w1q_l2", [8, 1024, 4096], BF16)
    w2q2 = p.dram("w2q_l2", [8, 4096, 1024], BF16)
    w1q3 = p.dram("w1q_l3", [8, 1024, 4096], BF16)
    w2q3 = p.dram("w2q_l3", [8, 4096, 1024], BF16)
    pc_stage = p.sb("pc_stage", [128, 2 * 1024], BF16)
    p.pcast = Ctr(p.es.enter_context(p.nc.semaphore("pcast")))
    p.pcast.nbuf = 0
    p.pcast.store_t = {}
    blk = p.es.enter_context(p.nc.Block())
    with p.es:
        @blk.sync
        def _(e):
            p.consts()
            with p.scope():
                stage = p.sb("stage", [128, 2 * D], F32)
                pb = p.ps("pb", [128, 1024])
                p.tE.wait_ge(p.sem("po").sem, p.sem("po").v)
                emit_load_x_fm(p, x_in, x_fm[:], stage[:], pb[:])
            p.join()

            def pc2():
                emit_moe_precast(p, v2["W1"], w1q2, 1024, 4096, pc_stage[:])
                emit_moe_precast(p, v2["W2"], w2q2, 4096, 1024, pc_stage[:])

            def pc3():
                emit_moe_precast(p, v3["W1"], w1q3, 1024, 4096, pc_stage[:])
                emit_moe_precast(p, v3["W2"], w2q3, 4096, 1024, pc_stage[:])

            emit_layer0_body(p, x_fm[:], v0, precast=pc2)
            pc2_done = p.pcast.v
            emit_layer1_body(p, x_fm[:], v1, precast=pc3)
            pc3_done = p.pcast.v
            emit_layer2_body(p, x_fm[:], v2, mask_fm[:],
                             moe_w=(w1q2, w2q2, pc2_done))
            emit_layer3_body(p, x_fm[:], v3, mask_fm[:], bb_sb[:],
                             moe_w=(w1q3, w2q3, pc3_done))
            with p.scope():
                stage = p.sb("stage2", [128, 2 * D], F32)
                pb = p.ps("pb2", [128, 1024])
                emit_store_fm(p, x_fm[:], x_out, stage[:], pb[:])
            p.join()
    return p.nc



def _c(a):
    return np.ascontiguousarray(np.asarray(a, np.float32))


def kernel(**inputs):
    x = _c(inputs["L0_attn_x_q"])
    B = x.shape[0]
    w = {
        "l0_w_q": _c(np.asarray(inputs["L0_attn_w_q"]).reshape(1024, 1024)),
        "l0_w_k": _c(np.asarray(inputs["L0_attn_w_k"]).reshape(1024, 1024)),
        "l0_w_v": _c(np.asarray(inputs["L0_attn_w_v"]).reshape(1024, 1024)),
        "l0_w_o": _c(inputs["L0_attn_w_o"]),
        "l0_w_gate": _c(inputs["L0_ffn_w_gate"]),
        "l0_w_up": _c(inputs["L0_ffn_w_up"]),
        "l0_w_down": _c(inputs["L0_ffn_w_down"]),
        "l1_qk": _c(inputs["L1_attn_qk"]), "l1_ov": _c(inputs["L1_attn_ov"]),
        "l1_w_gate": _c(inputs["L1_ffn_w_gate"]),
        "l1_w_up": _c(inputs["L1_ffn_w_up"]),
        "l1_w_down": _c(inputs["L1_ffn_w_down"]),
        "l2_Wq": _c(inputs["L2_attn_Wq"]), "l2_Wk": _c(inputs["L2_attn_Wk"]),
        "l2_Wv": _c(inputs["L2_attn_Wv"]), "l2_Wg": _c(inputs["L2_attn_Wg"]),
        "l2_Wgt": _c(inputs["L2_attn_Wgt"]), "l2_Wo": _c(inputs["L2_attn_Wo"]),
        "l2_router_w": _c(inputs["L2_ffn_router_w"]),
        "l2_router_b": _c(inputs["L2_ffn_router_b"]),
        "l2_W1": _c(inputs["L2_ffn_W1"]), "l2_b1": _c(inputs["L2_ffn_b1"]),
        "l2_W2": _c(inputs["L2_ffn_W2"]), "l2_b2": _c(inputs["L2_ffn_b2"]),
        "l3_Wq": _c(inputs["L3_attn_Wq"]), "l3_Wk": _c(inputs["L3_attn_Wk"]),
        "l3_Wv": _c(inputs["L3_attn_Wv"]), "l3_Wbeta": _c(inputs["L3_attn_Wbeta"]),
        "l3_b_beta": _c(inputs["L3_attn_b_beta"]), "l3_Wout": _c(inputs["L3_attn_Wout"]),
        "l3_router_w": _c(inputs["L3_ffn_router_W"]),
        "l3_router_b": _c(inputs["L3_ffn_router_b"]),
        "l3_W1": _c(inputs["L3_ffn_W1"]), "l3_b1": _c(inputs["L3_ffn_b1"]),
        "l3_W2": _c(inputs["L3_ffn_W2"]), "l3_b2": _c(inputs["L3_ffn_b2"]),
    }
    nc = build_fused()
    res = run_bass_kernel_spmd(nc, [dict(w, x=_c(x[b])) for b in range(B)],
                               core_ids=list(range(B)))
    out = np.stack([np.asarray(res.results[b]["x_out"]) for b in range(B)])
    return out.astype(np.float32)
